# version 30
# baseline (speedup 1.0000x reference)
import contextlib
import numpy as np
import ml_dtypes
import concourse.bass as bass
import concourse.mybir as mybir
from concourse.bass_utils import run_bass_kernel_spmd

F32 = mybir.dt.float32
BF16 = mybir.dt.bfloat16
AF = mybir.ActivationFunctionType
ALU = mybir.AluOpType
AX = mybir.AxisListType

S = 4096
C = 1024
NG = 8
DK = 256
EPS = 1e-5
BUILD_PHASE = 2

SWT_WQ = 16 * 8
SWT_WKV = 16 * 16
SWT_WF = 16 * 24
SWT_WO = 16 * 8
SSM_MU = 16 * 16
SSM_CONST = 16 * 3


def build_nc():
    nc = bass.Bass("TRN2", target_bir_lowering=False)

    x_d = nc.dram_tensor("x", [S, C], F32, kind="ExternalInput").ap()
    wq_d = nc.dram_tensor("wq", [C, C], F32, kind="ExternalInput").ap()
    wk_d = nc.dram_tensor("wk", [DK, C], F32, kind="ExternalInput").ap()
    wv_d = nc.dram_tensor("wv", [DK, C], F32, kind="ExternalInput").ap()
    wo_d = nc.dram_tensor("wo", [C, C], F32, kind="ExternalInput").ap()
    wr_d = nc.dram_tensor("wr", [C, C], F32, kind="ExternalInput").ap()
    wkk_d = nc.dram_tensor("wkk", [C, C], F32, kind="ExternalInput").ap()
    wvv_d = nc.dram_tensor("wvv", [C, C], F32, kind="ExternalInput").ap()
    mur_d = nc.dram_tensor("mur", [C], F32, kind="ExternalInput").ap()
    muk_d = nc.dram_tensor("muk", [C], F32, kind="ExternalInput").ap()
    masks_d = nc.dram_tensor("masks", [128, 896], BF16, kind="ExternalInput").ap()
    ident_d = nc.dram_tensor("ident", [128, 128], BF16, kind="ExternalInput").ap()
    asmb_d = nc.dram_tensor("asmb", [128, 128], BF16, kind="ExternalInput").ap()
    out_d = nc.dram_tensor("out", [S, C], F32, kind="ExternalOutput").ap()

    whq = nc.dram_tensor("whq", [C, C], BF16).ap()
    whk = nc.dram_tensor("whk", [DK, C], BF16).ap()
    whv = nc.dram_tensor("whv", [DK, C], BF16).ap()
    who = nc.dram_tensor("who", [C, C], BF16).ap()
    whr = nc.dram_tensor("whr", [C, C], BF16).ap()
    whkk = nc.dram_tensor("whkk", [C, C], BF16).ap()
    whvv = nc.dram_tensor("whvv", [C, C], BF16).ap()
    qscr = nc.dram_tensor("qscr", [8, 128, S], BF16).ap()

    with contextlib.ExitStack() as es:
        E = es.enter_context
        wqT = E(nc.sbuf_tensor("wqT", [128, 8192], BF16))
        wkT = E(nc.sbuf_tensor("wkT", [128, 2048], BF16))
        wvT = E(nc.sbuf_tensor("wvT", [128, 2048], BF16))
        woT = E(nc.sbuf_tensor("woT", [128, 8192], BF16))
        wrT = E(nc.sbuf_tensor("wrT", [128, 8192], BF16))
        wkkT = E(nc.sbuf_tensor("wkkT", [128, 8192], BF16))
        wvvT = E(nc.sbuf_tensor("wvvT", [128, 8192], BF16))
        kf = E(nc.sbuf_tensor("kf", [128, 8192], BF16))
        vp = E(nc.sbuf_tensor("vp", [128, 8320], BF16))
        qsl = E(nc.sbuf_tensor("qsl", [128, 4096], BF16))
        xnw = E(nc.sbuf_tensor("xnw", [128, 4104], BF16))
        masks = E(nc.sbuf_tensor("masks_sb", [128, 896], BF16))
        ident = E(nc.sbuf_tensor("ident_sb", [128, 128], BF16))
        asmb = E(nc.sbuf_tensor("asmb_sb", [128, 128], BF16))
        onesr = E(nc.sbuf_tensor("onesr", [128, 128], BF16))
        xstage = E(nc.sbuf_tensor("xstage", [128, 5120], F32))
        qstage = E(nc.sbuf_tensor("qstage", [128, 1024], BF16))
        xr = E(nc.sbuf_tensor("xr", [128, 2048], BF16))
        xk = E(nc.sbuf_tensor("xk", [128, 2048], BF16))
        kact = E(nc.sbuf_tensor("kact", [128, 2048], BF16))
        mixt = E(nc.sbuf_tensor("mixt", [128, 256], BF16))
        ktmp = E(nc.sbuf_tensor("ktmp", [128, 1024], F32))
        rtok = E(nc.sbuf_tensor("rtok", [128, 1024], F32))
        ffnb = E(nc.sbuf_tensor("ffnb", [128, 1024], F32))
        ffnb2 = E(nc.sbuf_tensor("ffnb2", [128, 1024], F32))
        probs01 = E(nc.sbuf_tensor("probs01", [128, 2048], BF16))
        probs0 = probs01[:, 0:1024]
        probs1 = probs01[:, 1024:2048]
        qsqs = [probs0[:, 0:512], probs0[:, 512:1024]]
        xntok = probs1[:, :]
        ctxf = E(nc.sbuf_tensor("ctxf", [128, 4096], BF16))
        ctxn = E(nc.sbuf_tensor("ctxn", [64, 1024], BF16))
        recip = E(nc.sbuf_tensor("recip", [128, 512], F32))
        rhil = E(nc.sbuf_tensor("rhil", [128, 1024], BF16))
        stt = E(nc.sbuf_tensor("stt", [128, 8], F32))
        sumx = E(nc.sbuf_tensor("sumx", [128, 1], F32))
        sumsq = E(nc.sbuf_tensor("sumsq", [128, 1], F32))
        meanc = E(nc.sbuf_tensor("meanc", [128, 1], F32))
        m2c = E(nc.sbuf_tensor("m2c", [128, 1], F32))
        varc = E(nc.sbuf_tensor("varc", [128, 1], F32))
        lnv = E(nc.sbuf_tensor("lnv", [128, 1], F32))
        rstd = E(nc.sbuf_tensor("rstd", [128, 1], F32))
        nmr = E(nc.sbuf_tensor("nmr", [128, 1], F32))
        epsc = E(nc.sbuf_tensor("epsc", [128, 1], F32))
        negc = E(nc.sbuf_tensor("negc", [128, 1], F32))
        negc1 = E(nc.sbuf_tensor("negc1", [128, 1], BF16))
        murc = E(nc.sbuf_tensor("murc", [128, 8], F32))
        mukc = E(nc.sbuf_tensor("mukc", [128, 8], F32))

        psS01 = E(nc.psum_tensor("psS01", [128, 2048], F32))
        psS0 = psS01[:, 0:1024]
        psS1 = psS01[:, 1024:2048]
        psCA = E(nc.psum_tensor("psCA", [128, 512], F32))
        psCB = E(nc.psum_tensor("psCB", [128, 512], F32))
        psM = E(nc.psum_tensor("psM", [128, 512], F32))
        psT = E(nc.psum_tensor("psT", [128, 1024], BF16))

        sem_names = [
            "s_x0", "s_x1", "s_x2", "s_x3", "s_x4", "s_w", "s_wf", "s_wo2",
            "s_wt_q", "s_wt_kv", "s_wt_f", "s_wt_o",
            "s_sm", "s_sm2", "s_q0", "s_q1", "s_qs", "s_p0", "s_p1",
            "s_pl0", "s_pl1", "s_pl2", "s_pl3", "s_out0", "s_out1",
            "a_ln", "d_ln", "c_tp", "c_tpe", "c_proj", "c_pevac", "c_qkv",
            "c_sm", "c_smr", "c_mix", "c_rp", "a_sig", "c_kp", "c_krelu",
            "c_vp", "c_ffnd", "c_sc", "c_exp", "c_msk2", "c_pv",
            "d_att", "p_att", "c_comb", "c_negc", "a_rc", "a_ev",
        ]
        sem_names.append("d_self")
        sem_names.append("a_self")
        sems = {n: E(nc.semaphore(n)) for n in sem_names}
        ctr = {n: 0 for n in sem_names}
        prog = []
        _dself = {"n": 0}
        _aself = {"n": 0}

        def SE(v, inst):
            _dself["n"] += 1
            inst.then_inc(sems["d_self"], 1)
            v.wait_ge(sems["d_self"], _dself["n"])

        def SEA(s, inst):
            _aself["n"] += 1
            inst.then_inc(sems["a_self"], 1)
            s.wait_ge(sems["a_self"], _aself["n"])

        def em(engine, fn):
            prog.append((engine, fn))

        def bump(name, n=1):
            ctr[name] += n
            return ctr[name]

        def pair_of(h):
            a, r = h // 8, h % 8
            return 4 * a + (r % 4), r // 4

        PS = [psS0, psS1]
        XNW = lambda: xnw[:, :].rearrange("p (c w) -> p c w", c=8)

        def setup_gpsimd(g):
            g.dma_start(whq[:, :], wq_d[:, :]).then_inc(sems["s_w"], 16)
            g.dma_start(whk[:, :], wk_d[:, :]).then_inc(sems["s_w"], 16)
            g.dma_start(whv[:, :], wv_d[:, :]).then_inc(sems["s_w"], 16)
            g.dma_start(whr[:, :], wr_d[:, :]).then_inc(sems["s_wf"], 16)
            g.dma_start(whkk[:, :], wkk_d[:, :]).then_inc(sems["s_wf"], 16)
            g.dma_start(whvv[:, :], wvv_d[:, :]).then_inc(sems["s_wf"], 16)
            for b in range(16):
                p, slot = b // 2, b % 2
                a, i = p // 4, p % 4
                h = 8 * a + 4 * slot + i
                g.dma_start(who[:, 64 * b:64 * b + 64],
                            wo_d[:, 64 * h:64 * h + 64]).then_inc(sems["s_wo2"], 16)
            for j in range(8):
                g.dma_start(murc[:, j:j + 1],
                            mur_d[128 * j:128 * j + 128].rearrange(
                                "(p one) -> p one", p=128, one=1)).then_inc(sems["s_sm"], 16)
                g.dma_start(mukc[:, j:j + 1],
                            muk_d[128 * j:128 * j + 128].rearrange(
                                "(p one) -> p one", p=128, one=1)).then_inc(sems["s_sm"], 16)

        def setup_sync(sy):
            sy.dma_start(masks[:, :], masks_d[:, :]).then_inc(sems["s_sm2"], 16)
            sy.dma_start(ident[:, :], ident_d[:, :]).then_inc(sems["s_sm2"], 16)
            sy.dma_start(asmb[:, :], asmb_d[:, :]).then_inc(sems["s_sm2"], 16)

        def setup_vector(v):
            v.memset(onesr[:, :], 1.0)
            v.memset(epsc[:, :], EPS)
            v.memset(stt[:, :], 0.0)
            v.memset(vp[:, :].rearrange("p (g k w) -> p g k w", g=4, k=32)[:, :, :, 64:65], 1.0)
            v.memset(XNW()[:, :, 0:1], 0.0)

        def emit_xf(src, dst, nchunk, rows, semn):
            for cc in range(nchunk):
                def fn(sy, src=src, dst=dst, cc=cc, rows=rows, semn=semn):
                    sy.dma_start(dst[:, rows * cc:rows * cc + rows],
                                 src[:, 128 * cc:128 * cc + 128],
                                 transpose=True).then_inc(sems[semn], 16)
                bump(semn, 16)
                em("sync", fn)

        def emit_x(t):
            def fn(sy, t=t):
                if t >= 5:
                    sy.wait_ge(sems["c_ffnd"], t - 4)
                sy.dma_start(xstage[:, 1024 * (t % 5):1024 * (t % 5) + 1024],
                             x_d[128 * t:128 * t + 128, :]).then_inc(sems["s_x%d" % (t % 5)], 16)
            bump("s_x%d" % (t % 5), 16)
            em("sync", fn)

        def emit_wait_sw(thr):
            em("sync", lambda sy, thr=thr: sy.wait_ge(sems["s_w"], thr))

        for t in range(4):
            emit_x(t)
        emit_wait_sw(16 * 3)
        emit_xf(whq, wqT, 8, 1024, "s_wt_q")
        emit_x(4)
        emit_xf(whk, wkT, 8, 256, "s_wt_kv")
        emit_xf(whv, wvT, 8, 256, "s_wt_kv")
        em("sync", lambda sy: sy.wait_ge(sems["s_wf"], 16 * 3))
        emit_xf(whr, wrT, 8, 1024, "s_wt_f")
        emit_xf(whkk, wkkT, 8, 1024, "s_wt_f")
        emit_xf(whvv, wvvT, 8, 1024, "s_wt_f")
        em("sync", lambda sy: sy.wait_ge(sems["s_wo2"], 16 * 16))
        emit_xf(who, woT, 8, 1024, "s_wt_o")
        for t in range(5, 32):
            emit_x(t)

        qevac_cnt = []
        qk_state = {"n": 0, "sm_vals": []}
        pend_stat = []

        def build_ln(g):
            carry = None
            if g >= 1:
                def f_dve_carry(v):
                    SE(v, v.tensor_copy(XNW()[:, :, 0:1], XNW()[:, :, 512:513]))
                carry = ("vector", f_dve_carry)
            tchains = []
            for j in range(4):
                t = 4 * g + j
                tj = []

                c_sm_snap = ctr["c_sm"]

                def f_act_sq(s, t=t, c_sm_snap=c_sm_snap):
                    s.wait_ge(sems["s_x%d" % (t % 5)], 16 * (t // 5 + 1))
                    if c_sm_snap > 0:
                        s.wait_ge(sems["c_sm"], c_sm_snap)
                    if t >= 1:
                        s.wait_ge(sems["d_ln"], 2 * t - 1)
                    s.activation(probs0[:, :], xstage[:, 1024 * (t % 5):1024 * (t % 5) + 1024],
                                 AF.Square, accum_out=sumsq[:, :]).then_inc(sems["a_ln"], 1)
                bump("a_ln")
                tj.append(("scalar", f_act_sq))
                a_sq = ctr["a_ln"]

                def f_dve_stats(v, t=t, a_sq=a_sq):
                    v.wait_ge(sems["s_x%d" % (t % 5)], 16 * (t // 5 + 1))
                    if t >= 1:
                        v.wait_ge(sems["a_ln"], a_sq - 1)
                    SE(v, v.reduce_sum(sumx[:, :],
                                       xstage[:, 1024 * (t % 5):1024 * (t % 5) + 1024],
                                       axis=AX.X))
                    SE(v, v.tensor_scalar(meanc[:, :], sumx[:, :], 1.0 / C, None, ALU.mult))
                    SE(v, v.tensor_tensor(m2c[:, :], meanc[:, :], meanc[:, :], ALU.mult))
                    v.wait_ge(sems["a_ln"], a_sq)
                    v.scalar_tensor_tensor(varc[:, :], sumsq[:, :], 1.0 / C, m2c[:, :],
                                           ALU.mult, ALU.subtract).then_inc(sems["d_ln"], 1)
                bump("d_ln")
                tj.append(("vector", f_dve_stats))
                d_var = ctr["d_ln"]

                def f_act_rstd(s, d_var=d_var):
                    s.wait_ge(sems["d_ln"], d_var)
                    SEA(s, s.activation(lnv[:, :], varc[:, :], AF.Ln, bias=epsc[:, :]))
                    s.activation(rstd[:, :], lnv[:, :], AF.Exp,
                                 scale=-0.5).then_inc(sems["a_ln"], 1)
                bump("a_ln")
                tj.append(("scalar", f_act_rstd))
                a_rstd = ctr["a_ln"]

                def f_dve_nmr(v, a_rstd=a_rstd):
                    v.wait_ge(sems["a_ln"], a_rstd)
                    v.scalar_tensor_tensor(nmr[:, :], meanc[:, :], -1.0, rstd[:, :],
                                           ALU.mult, ALU.mult).then_inc(sems["d_ln"], 1)
                bump("d_ln")
                tj.append(("vector", f_dve_nmr))
                d_nmr = ctr["d_ln"]

                def f_act_apply(s, t=t, d_nmr=d_nmr):
                    s.wait_ge(sems["d_ln"], d_nmr)
                    if t >= 1:
                        s.wait_ge(sems["c_tp"], t)
                    s.activation(xntok[:, :], xstage[:, 1024 * (t % 5):1024 * (t % 5) + 1024],
                                 AF.Identity, bias=nmr[:, :],
                                 scale=rstd[:, :]).then_inc(sems["a_ln"], 1)
                bump("a_ln")
                tj.append(("scalar", f_act_apply))
                a_app = ctr["a_ln"]

                def f_pe_tp(pe, t=t, a_app=a_app):
                    pe.wait_ge(sems["a_ln"], a_app)
                    pe.wait_ge(sems["s_sm2"], SSM_CONST)
                    if t >= 1:
                        pe.wait_ge(sems["c_tpe"], t)
                    for cc in range(8):
                        inst = pe.transpose(psT[:, 128 * cc:128 * cc + 128],
                                            xntok[:, 128 * cc:128 * cc + 128], ident[:, :])
                    inst.then_inc(sems["c_tp"], 1)
                bump("c_tp")
                tj.append(("tensor", f_pe_tp))
                c_tp_t = ctr["c_tp"]

                def f_dve_evac(v, g=g, j=j, c_tp_t=c_tp_t):
                    v.wait_ge(sems["c_tp"], c_tp_t)
                    if j == 0 and g >= 1:
                        v.wait_ge(sems["c_qkv"], g)
                    dst = XNW()[:, :, 1 + 128 * j:1 + 128 * j + 128]
                    src = psT[:, :].rearrange("p (c w) -> p c w", c=8)
                    v.tensor_copy(dst, src).then_inc(sems["c_tpe"], 1)
                bump("c_tpe")
                tj.append(("vector", f_dve_evac))
                tchains.append(tj)
            return carry, tchains

        ln0_carry, ln0_t = build_ln(0)
        for tj in ln0_t:
            for it in tj:
                em(*it)

        for g in range(NG):
            krelu_grp_snap = ctr["c_krelu"]
            c_tpe_g = ctr["c_tpe"]

            mix0_base = ctr["c_mix"]
            mix0_items = []
            c_rp_s0 = ctr["c_rp"]
            c_kp_s0 = ctr["c_kp"]
            for cc in range(8):
                def f_dve_mix0(v, cc=cc, c_tpe_g=c_tpe_g, c_rp_s0=c_rp_s0,
                               c_kp_s0=c_kp_s0, mix0_base=mix0_base):
                    if cc == 0:
                        v.wait_ge(sems["c_tpe"], c_tpe_g)
                        v.wait_ge(sems["s_sm"], SSM_MU)
                        if c_rp_s0 > 0:
                            v.wait_ge(sems["c_rp"], c_rp_s0)
                        if c_kp_s0 > 0:
                            v.wait_ge(sems["c_kp"], c_kp_s0)
                    else:
                        v.wait_ge(sems["c_mix"], mix0_base + 2 * cc)
                    cur = XNW()[:, cc, 1:257]
                    prv = XNW()[:, cc, 0:256]
                    SE(v, v.tensor_tensor(mixt[:, :], cur, prv, ALU.subtract))
                    v.scalar_tensor_tensor(xr[:, 256 * cc:256 * cc + 256], mixt[:, :],
                                           murc[:, cc:cc + 1], prv,
                                           ALU.mult, ALU.add).then_inc(sems["c_mix"], 1)
                    v.scalar_tensor_tensor(xk[:, 256 * cc:256 * cc + 256], mixt[:, :],
                                           mukc[:, cc:cc + 1], prv,
                                           ALU.mult, ALU.add).then_inc(sems["c_mix"], 1)
                bump("c_mix", 2)
                mix0_items.append(("vector", f_dve_mix0))
            mix0_h = ctr["c_mix"]

            group_units = [("q", j) for j in range(8)] + \
                          [("k", a) for a in range(2)] + \
                          [("v", u) for u in range(4)]

            for ui, (kind, idx) in enumerate(group_units):
                unit_no = len(qevac_cnt)
                quarters = [(psS0, 0), (psS0, 512), (psS1, 0), (psS1, 512),
                            (psCA, 0), (psCB, 0)]
                half, hoff = quarters[unit_no % 6]

                def f_pe_proj(pe, g=g, kind=kind, idx=idx, half=half, hoff=hoff,
                              unit_no=unit_no, c_tpe_g=c_tpe_g, ui=ui,
                              krelu_grp_snap=krelu_grp_snap):
                    if unit_no == 0:
                        pe.wait_ge(sems["s_wt_q"], SWT_WQ)
                    if unit_no == 8:
                        pe.wait_ge(sems["s_wt_kv"], SWT_WKV)
                    pe.wait_ge(sems["c_tpe"], c_tpe_g)
                    if ui < 6 and g >= 1:
                        pe.wait_ge(sems["c_ffnd"], 4 * g)
                        pe.wait_ge(sems["c_krelu"], krelu_grp_snap)
                    if unit_no >= 6:
                        pe.wait_ge(sems["c_pevac"], qevac_cnt[unit_no - 6])
                    if kind == "q":
                        for cc in range(8):
                            inst = pe.matmul(half[:, hoff:hoff + 512],
                                      wqT[:, 1024 * cc + 128 * idx:1024 * cc + 128 * idx + 128],
                                      XNW()[:, cc, 1:513],
                                      start=(cc == 0), stop=(cc == 7))
                    elif kind == "k":
                        for cc in range(8):
                            inst = pe.matmul(half[:, hoff:hoff + 512],
                                      wkT[:, 256 * cc + 128 * idx:256 * cc + 128 * idx + 128],
                                      XNW()[:, cc, 1:513],
                                      start=(cc == 0), stop=(cc == 7))
                    else:
                        for cc in range(8):
                            inst = pe.matmul(half[:, hoff:hoff + 256],
                                      XNW()[:, cc, 1 + 128 * idx:1 + 128 * idx + 128],
                                      wvT[:, 256 * cc:256 * cc + 256],
                                      start=(cc == 0), stop=(cc == 7))
                    inst.then_inc(sems["c_proj"], 1)
                bump("c_proj")
                em("tensor", f_pe_proj)
                c_proj_u = ctr["c_proj"]

                if kind in ("q", "k"):
                    jg = 8 * g + idx if kind == "q" else None
                    qk_no = qk_state["n"]
                    qk_state["n"] += 1
                    qsq = qsqs[qk_no % 2]
                    sm_prev = (qk_state["sm_vals"][qk_no - 2]
                               if qk_no >= 2 else 0)
                    c_pev_u = ctr["c_pevac"] + 2

                    def f_dve_pevac(v, g=g, kind=kind, idx=idx, half=half, hoff=hoff,
                                    c_proj_u=c_proj_u, jg=jg, sm_prev=sm_prev,
                                    c_pev_u=c_pev_u, qsq=qsq):
                        v.wait_ge(sems["c_proj"], c_proj_u)
                        if sm_prev > 0:
                            v.wait_ge(sems["c_sm"], sm_prev)
                        if kind == "q":
                            if jg >= 2:
                                v.wait_ge(sems["s_q%d" % (jg % 2)], 32 * (jg // 2))
                            q_half = qstage[:, 512 * (jg % 2):512 * (jg % 2) + 512]
                            v.tensor_scalar(q_half, half[:, hoff:hoff + 512],
                                            0.125, None,
                                            ALU.mult).then_inc(sems["c_pevac"], 1)
                            v.wait_ge(sems["c_pevac"], c_pev_u - 1)
                            v.tensor_tensor(qsq[:, :], q_half, q_half,
                                            ALU.mult).then_inc(sems["c_pevac"], 1)
                        else:
                            kdst = kf[:, 4096 * idx + 512 * g:4096 * idx + 512 * g + 512]
                            v.tensor_copy(kdst,
                                          half[:, hoff:hoff + 512]).then_inc(sems["c_pevac"], 1)
                            v.wait_ge(sems["c_pevac"], c_pev_u - 1)
                            v.tensor_tensor(qsq[:, :], kdst, kdst,
                                            ALU.mult).then_inc(sems["c_pevac"], 1)
                    bump("c_pevac", 2)
                    em("vector", f_dve_pevac)
                    assert c_pev_u == ctr["c_pevac"]
                    c_smr_snap = ctr["c_smr"]
                    c_smr_fin = ctr["c_smr"] + 2

                    def f_pe_stat(pe, c_pev_u=c_pev_u, c_smr_snap=c_smr_snap, qsq=qsq):
                        pe.wait_ge(sems["c_pevac"], c_pev_u)
                        pe.wait_ge(sems["c_smr"], c_smr_snap)
                        pe.matmul(psM[0:1, 0:512], onesr[:, 0:1], qsq[:, :],
                                  start=True, stop=True).then_inc(sems["c_sm"], 1)
                    bump("c_sm")
                    qk_state["sm_vals"].append(ctr["c_sm"])
                    c_sm_u = ctr["c_sm"]
                    col = 0 if kind == "q" else 1

                    def f_dve_stat(v, c_sm_u=c_sm_u, col=col, c_smr_fin=c_smr_fin):
                        v.wait_ge(sems["c_sm"], c_sm_u)
                        v.reduce_max(stt[0:1, 2:3], psM[0:1, 0:512],
                                     axis=AX.X).then_inc(sems["c_smr"], 1)
                        v.wait_ge(sems["c_smr"], c_smr_fin - 1)
                        v.tensor_tensor(stt[0:1, col:col + 1], stt[0:1, col:col + 1],
                                        stt[0:1, 2:3], ALU.max).then_inc(sems["c_smr"], 1)
                        v.wait_ge(sems["c_smr"], c_smr_fin)
                    bump("c_smr", 2)
                    while pend_stat:
                        em(*pend_stat.pop(0))
                    pend_stat.append(("tensor", f_pe_stat))
                    pend_stat.append(("vector", f_dve_stat))

                    if kind == "q":
                        def f_gp_qstore(gp, g=g, idx=idx, jg=jg, c_pev_u=c_pev_u):
                            gp.wait_ge(sems["c_pevac"], c_pev_u)
                            for hh, h in enumerate((2 * idx, 2 * idx + 1)):
                                p, slot = pair_of(h)
                                gp.dma_start(
                                    qscr[p, 64 * slot:64 * slot + 64, 512 * g:512 * g + 512],
                                    qstage[64 * hh:64 * hh + 64,
                                           512 * (jg % 2):512 * (jg % 2) + 512]
                                ).then_inc(sems["s_q%d" % (jg % 2)], 16)
                        bump("s_q%d" % (jg % 2), 32)
                        em("gpsimd", f_gp_qstore)
                else:
                    def f_dve_vevac(v, g=g, idx=idx, half=half, hoff=hoff, c_proj_u=c_proj_u):
                        v.wait_ge(sems["c_proj"], c_proj_u)
                        kj = 4 * g + idx
                        dst = vp[:, :].rearrange("p (gg k w) -> p gg k w",
                                                 gg=4, k=32)[:, :, kj, 0:64]
                        src = half[:, hoff:hoff + 256].rearrange("p (gg w) -> p gg w", gg=4)
                        v.tensor_copy(dst, src).then_inc(sems["c_pevac"], 1)
                    bump("c_pevac")
                    em("vector", f_dve_vevac)
                qevac_cnt.append(ctr["c_pevac"])
                if mix0_items:
                    em(*mix0_items.pop(0))

            grp_evac_all = ctr["c_pevac"]
            while mix0_items:
                em(*mix0_items.pop(0))
            while pend_stat:
                em(*pend_stat.pop(0))

            def f_pe_qkv_done(pe):
                pe.nop().then_inc(sems["c_qkv"], 1)
            bump("c_qkv")
            em("tensor", f_pe_qkv_done)

            ln_next = build_ln(g + 1) if g + 1 < NG else None
            for hg in range(2):
                spliced = ln_next is not None and hg == 1
                LT = ln_next[1] if ln_next is not None else None
                c_rp_snap = ctr["c_rp"]
                c_kp_snap = ctr["c_kp"]
                if hg == 0:
                    c_mix_h = mix0_h
                    c_vp_half_snap = ctr["c_vp"]
                    do_mix = False
                else:
                    do_mix = True
                if do_mix:
                    def f_dve_mix(v, g=g, hg=hg, c_tpe_g=c_tpe_g,
                                  c_rp_snap=c_rp_snap, c_kp_snap=c_kp_snap,
                                  c_mix_h=ctr["c_mix"] + 16):
                        v.wait_ge(sems["c_tpe"], c_tpe_g)
                        v.wait_ge(sems["s_sm"], SSM_MU)
                        if c_rp_snap > 0:
                            v.wait_ge(sems["c_rp"], c_rp_snap)
                        if c_kp_snap > 0:
                            v.wait_ge(sems["c_kp"], c_kp_snap)
                        for cc in range(8):
                            cur = XNW()[:, cc, 1 + 256 * hg:1 + 256 * hg + 256]
                            prv = XNW()[:, cc, 0 + 256 * hg:0 + 256 * hg + 256]
                            if cc >= 1:
                                v.wait_ge(sems["c_mix"], c_mix_h - 16 + 2 * cc)
                            SE(v, v.tensor_tensor(mixt[:, :], cur, prv, ALU.subtract))
                            v.scalar_tensor_tensor(xr[:, 256 * cc:256 * cc + 256], mixt[:, :],
                                                   murc[:, cc:cc + 1], prv,
                                                   ALU.mult, ALU.add).then_inc(sems["c_mix"], 1)
                            v.scalar_tensor_tensor(xk[:, 256 * cc:256 * cc + 256], mixt[:, :],
                                                   mukc[:, cc:cc + 1], prv,
                                                   ALU.mult, ALU.add).then_inc(sems["c_mix"], 1)
                    bump("c_mix", 16)
                    em("vector", f_dve_mix)
                    if spliced and ln_next[0] is not None:
                        em(*ln_next[0])
                    c_mix_h = ctr["c_mix"]
                    c_vp_half_snap = ctr["c_vp"]

                for kw in range(2):
                    c_krelu_snap = ctr["c_krelu"]

                    def f_pe_k(pe, g=g, hg=hg, kw=kw, c_mix_h=c_mix_h,
                               c_krelu_snap=c_krelu_snap, grp_evac_all=grp_evac_all):
                        if g == 0 and hg == 0 and kw == 0:
                            pe.wait_ge(sems["s_wt_f"], SWT_WF)
                        if kw == 0:
                            pe.wait_ge(sems["c_pevac"], grp_evac_all)
                        if c_krelu_snap > 0:
                            pe.wait_ge(sems["c_krelu"], c_krelu_snap - 1)
                        ktiles = [psS1[:, 0:256], psS1[:, 512:768],
                                  psCA[:, 0:256], psCB[:, 0:256]]
                        for cc in range(8):
                            pe.wait_ge(sems["c_mix"], c_mix_h - 16 + 2 * cc + 2)
                            for dk4 in range(4):
                                dk = 4 * kw + dk4
                                inst = pe.matmul(ktiles[dk4],
                                          wkkT[:, 1024 * cc + 128 * dk:1024 * cc + 128 * dk + 128],
                                          xk[:, 256 * cc:256 * cc + 256],
                                          start=(cc == 0), stop=(cc == 7))
                        inst.then_inc(sems["c_kp"], 1)
                    bump("c_kp")
                    if spliced:
                        em(*LT[kw][0])
                        em(*LT[kw][2])
                    em("tensor", f_pe_k)
                    c_kp_w = ctr["c_kp"]

                    krelu_fin = ctr["c_krelu"] + 5

                    def f_dve_krelu(v, kw=kw, c_kp_w=c_kp_w, c_vp_half_snap=c_vp_half_snap,
                                    krelu_fin=krelu_fin):
                        v.wait_ge(sems["c_kp"], c_kp_w)
                        if krelu_fin > 5:
                            v.wait_ge(sems["c_krelu"], krelu_fin - 5)
                        if c_vp_half_snap > 0:
                            v.wait_ge(sems["c_vp"], c_vp_half_snap)
                        ktiles = [psS1[:, 0:256], psS1[:, 512:768],
                                  psCA[:, 0:256], psCB[:, 0:256]]
                        for qd in range(4):
                            v.tensor_scalar(ktmp[:, 256 * qd:256 * qd + 256], ktiles[qd],
                                            0.0, None,
                                            ALU.max).then_inc(sems["c_krelu"], 1)
                        v.wait_ge(sems["c_krelu"], krelu_fin - 1)
                        v.tensor_tensor(kact[:, 1024 * kw:1024 * kw + 1024],
                                        ktmp[:, :], ktmp[:, :],
                                        ALU.mult).then_inc(sems["c_krelu"], 1)
                    bump("c_krelu", 5)
                    em("vector", f_dve_krelu)
                    if spliced:
                        em(*LT[kw][1])
                        em(*LT[kw][3])
                        em(*LT[kw][4])
                if spliced:
                    em(*LT[0][5])
                c_krelu_h = ctr["c_krelu"]

                for mm in range(2):
                    m = 2 * hg + mm
                    mi = 4 * g + m

                    def f_pe_r(pe, g=g, m=m, mm=mm, mi=mi, c_mix_h=c_mix_h,
                               grp_evac_all=grp_evac_all):
                        if g == 0 and m == 0:
                            pe.wait_ge(sems["s_wt_f"], SWT_WF)
                        if m == 0:
                            pe.wait_ge(sems["c_pevac"], grp_evac_all)
                        if mi >= 1:
                            pe.wait_ge(sems["c_ffnd"], mi)
                        for cc in range(8):
                            pe.wait_ge(sems["c_mix"], c_mix_h - 16 + 2 * cc + 1)
                            for hh in range(2):
                                inst = pe.matmul(psS0[:, 512 * hh:512 * hh + 512],
                                          xr[:, 256 * cc + 128 * mm:256 * cc + 128 * mm + 128],
                                          wrT[:, 1024 * cc + 512 * hh:1024 * cc + 512 * hh + 512],
                                          start=(cc == 0), stop=(cc == 7))
                        inst.then_inc(sems["c_rp"], 1)
                    bump("c_rp")
                    em("tensor", f_pe_r)
                    if spliced and mm == 0:
                        em(*LT[1][5])
                        em(*LT[0][6])
                    c_rp_m = ctr["c_rp"]

                    def f_act_sig(s, c_rp_m=c_rp_m, a1=ctr["a_sig"] + 1):
                        s.wait_ge(sems["c_rp"], c_rp_m)
                        s.activation(rtok[:, :], psS0[:, :], AF.Exp,
                                     scale=-1.0).then_inc(sems["a_sig"], 1)
                        s.wait_ge(sems["a_sig"], a1)
                        SEA(s, s.activation(rtok[:, :], rtok[:, :], AF.Ln,
                                            bias=1.0))
                        s.activation(rtok[:, :], rtok[:, :], AF.Exp,
                                     scale=-1.0).then_inc(sems["a_sig"], 1)
                    bump("a_sig", 2)
                    em("scalar", f_act_sig)
                    a_sig_m = ctr["a_sig"] - 1
                    if spliced:
                        if mm == 0:
                            em(*LT[1][6])
                        tn = 2 + mm
                        for it in (LT[tn][0], LT[tn][2], LT[tn][1], LT[tn][3],
                                   LT[tn][4]):
                            em(*it)
                    a_sig2_m = ctr["a_sig"]

                    def f_pe_vout(pe, g=g, mi=mi, mm=mm, c_krelu_h=c_krelu_h,
                                  a_sig_m=a_sig_m):
                        if g == 0 and mi == 0:
                            pe.wait_ge(sems["s_wt_f"], SWT_WF)
                        pe.wait_ge(sems["c_krelu"], c_krelu_h)
                        pe.wait_ge(sems["a_sig"], a_sig_m)
                        for hh in range(2):
                            for dk in range(8):
                                inst = pe.matmul(psS0[:, 512 * hh:512 * hh + 512],
                                          kact[:, 256 * dk + 128 * mm:256 * dk + 128 * mm + 128],
                                          wvvT[:, 1024 * dk + 512 * hh:1024 * dk + 512 * hh + 512],
                                          start=(dk == 0), stop=(dk == 7))
                        inst.then_inc(sems["c_vp"], 1)
                    bump("c_vp")
                    em("tensor", f_pe_vout)
                    if spliced:
                        em(*LT[2 + mm][5])
                    c_vp_m = ctr["c_vp"]
                    fb = ffnb if mi % 2 == 0 else ffnb2

                    def f_dve_ffn(v, mi=mi, fb=fb, c_vp_m=c_vp_m, a_sig2_m=a_sig2_m):
                        v.wait_ge(sems["c_vp"], c_vp_m)
                        v.wait_ge(sems["a_sig"], a_sig2_m)
                        if mi >= 2:
                            v.wait_ge(sems["s_p%d" % (mi % 2)], 16 * (mi // 2))
                        SE(v, v.tensor_tensor(fb[:, :], psS0[:, :], rtok[:, :], ALU.mult))
                        v.tensor_tensor(fb[:, :], fb[:, :],
                                        xstage[:, 1024 * (mi % 5):1024 * (mi % 5) + 1024],
                                        ALU.add).then_inc(sems["c_ffnd"], 1)
                    bump("c_ffnd")
                    em("vector", f_dve_ffn)
                    c_ffnd_m = ctr["c_ffnd"]

                    def f_gp_pstore(gp, mi=mi, fb=fb, c_ffnd_m=c_ffnd_m):
                        gp.wait_ge(sems["c_ffnd"], c_ffnd_m)
                        gp.dma_start(out_d[128 * mi:128 * mi + 128, :],
                                     fb[:, :]).then_inc(sems["s_p%d" % (mi % 2)], 16)
                    bump("s_p%d" % (mi % 2), 16)
                    em("gpsimd", f_gp_pstore)
                    if spliced:
                        em(*LT[2 + mm][6])

        def f_dve_cprep(v):
            v.tensor_tensor(stt[0:1, 3:4], stt[0:1, 0:1], stt[0:1, 1:2],
                            ALU.mult).then_inc(sems["d_att"], 1)
        bump("d_att")
        em("vector", f_dve_cprep)
        d_cp = ctr["d_att"]

        def f_act_c(s, d_cp=d_cp):
            s.wait_ge(sems["d_att"], d_cp)
            SEA(s, s.activation(stt[0:1, 4:5], stt[0:1, 3:4], AF.Ln))
            s.activation(stt[0:1, 5:6], stt[0:1, 4:5], AF.Exp,
                         scale=0.5).then_inc(sems["a_ln"], 1)
        bump("a_ln")
        em("scalar", f_act_c)
        a_c = ctr["a_ln"]

        def f_dve_negc1(v, a_c=a_c):
            v.wait_ge(sems["a_ln"], a_c)
            v.tensor_scalar(negc1[0:1, 0:1], stt[0:1, 5:6], -1.0, -0.5,
                            ALU.mult, ALU.add).then_inc(sems["d_att"], 1)
        bump("d_att")
        em("vector", f_dve_negc1)
        d_nc1 = ctr["d_att"]

        def f_pe_negc(pe, d_nc1=d_nc1):
            pe.wait_ge(sems["d_att"], d_nc1)
            pe.matmul(psM[0:128, 0:1], onesr[0:1, :], negc1[0:1, 0:1],
                      start=True, stop=True).then_inc(sems["p_att"], 1)
        bump("p_att")
        em("tensor", f_pe_negc)
        p_nc = ctr["p_att"]

        def f_dve_negc(v, p_nc=p_nc):
            v.wait_ge(sems["p_att"], p_nc)
            v.tensor_copy(negc[:, :], psM[:, 0:1]).then_inc(sems["c_negc"], 1)
        bump("c_negc")
        em("vector", f_dve_negc)

        PH2 = BUILD_PHASE >= 2
        wave = 0
        pair_gl = 0
        pend_lazy = []
        d_evac_prev = [0]
        d_norm_prev = [0]
        exp_at_wave = []
        pv_at_wave = []

        pend_epi = []
        sc_pair_snap = []
        for sg in range(8 if PH2 else 0):
            for pr in range(8):
                qs_prev = ctr["s_qs"]

                def f_sy_qsl(sy, sg=sg, pr=pr, qs_prev=qs_prev,
                             sc_snap=(sc_pair_snap[8 * (sg - 1) + pr] if sg >= 1 else 0)):
                    if sg == 0 and pr == 0:
                        sy.wait_ge(sems["s_q0"], 32 * 32)
                        sy.wait_ge(sems["s_q1"], 32 * 32)
                    if qs_prev > 0:
                        sy.wait_ge(sems["s_qs"], qs_prev)
                    if sg >= 1:
                        sy.wait_ge(sems["c_sc"], sc_snap)
                    sy.dma_start(
                        qsl[:, 512 * pr:512 * pr + 512],
                        qscr[pr:pr + 1, :, 512 * sg:512 * sg + 512].rearrange(
                            "p r c -> r (p c)")).then_inc(sems["s_qs"], 16)
                bump("s_qs", 16)
                em("sync", f_sy_qsl)
            s_qs_sg = ctr["s_qs"]

            for u in range(4):
                def f_sy_pl(sy, sg=sg, u=u):
                    if sg == 0 and u == 0:
                        sy.wait_ge(sems["s_p0"], 16 * 16)
                        sy.wait_ge(sems["s_p1"], 16 * 16)
                        sy.wait_ge(sems["c_ffnd"], 32)
                    if sg >= 1:
                        sy.wait_ge(sems["c_comb"], 4 * (sg - 1) + u + 1)
                    sy.dma_start(xstage[:, 1024 * u:1024 * u + 1024],
                                 out_d[128 * (4 * sg + u):128 * (4 * sg + u) + 128, :]
                                 ).then_inc(sems["s_pl%d" % u], 16)
                bump("s_pl%d" % u, 16)
                em("sync", f_sy_pl)

            asme_vals = []
            for p in range(8):
                a = p // 4
                gA, gB = 2 * a, 2 * a + 1
                nkj = 4 * sg + 4
                d_att_pair = d_evac_prev[0]
                sc_fns = []
                pv_fns = []
                exp_fns = []
                msk_fns = []
                for kj in range(nkj):
                    w = wave
                    pi = w % 2
                    diag = kj >= 4 * sg
                    dd = 128 * kj - 512 * sg if diag else 0

                    def f_pe_sc(pe, w=w, pi=pi, p=p, a=a, kj=kj, sg=sg,
                                s_qs_sg=s_qs_sg):
                        if w == 0:
                            pe.wait_ge(sems["c_negc"], 1)
                        pe.wait_ge(sems["s_qs"], 16 * (8 * sg + p + 1))
                        if w >= 2:
                            pe.wait_ge(sems["c_exp"], exp_at_wave[w - 2])
                        if p == 0 and kj <= 1 and sg >= 1:
                            pe.wait_ge(sems["c_comb"], 4 * sg)
                        ps = PS[pi]
                        pe.matmul(ps[:, 0:512],
                                  kf[0:64, 4096 * a + 128 * kj:4096 * a + 128 * kj + 128],
                                  qsl[0:64, 512 * p:512 * p + 512], start=True, stop=True)
                        pe.matmul(ps[:, 512:1024],
                                  kf[64:128, 4096 * a + 128 * kj:4096 * a + 128 * kj + 128],
                                  qsl[64:128, 512 * p:512 * p + 512],
                                  start=True, stop=True).then_inc(sems["c_sc"], 1)
                    bump("c_sc")
                    sc_fns.append(f_pe_sc)
                    c_sc_w = ctr["c_sc"]

                    def f_act_exp(s, w=w, pi=pi, c_sc_w=c_sc_w):
                        if w == 0:
                            s.wait_ge(sems["c_negc"], 1)
                        s.wait_ge(sems["c_sc"], c_sc_w)
                        if w >= 2:
                            s.wait_ge(sems["c_pv"], pv_at_wave[w - 2])
                        pb = probs0 if w % 2 == 0 else probs1
                        s.activation(pb[:, :], PS[pi][:, :], AF.Exp,
                                     bias=negc[:, :]).then_inc(sems["c_exp"], 1)
                    bump("c_exp")
                    exp_fns.append(f_act_exp)
                    exp_at_wave.append(ctr["c_exp"])
                    c_exp_w = ctr["c_exp"]

                    if diag:
                        def f_dve_mask(v, w=w, dd=dd, c_exp_w=c_exp_w):
                            v.wait_ge(sems["c_exp"], c_exp_w)
                            if w < 8:
                                v.wait_ge(sems["s_sm2"], SSM_CONST)
                            pb = probs0 if w % 2 == 0 else probs1
                            mv = masks[:, 384 - dd:384 - dd + 512]
                            v.tensor_tensor(pb[:, 0:512], pb[:, 0:512], mv,
                                            ALU.mult).then_inc(sems["c_msk2"], 1)
                            v.tensor_tensor(pb[:, 512:1024], pb[:, 512:1024], mv,
                                            ALU.mult).then_inc(sems["c_msk2"], 1)
                        bump("c_msk2", 2)
                        msk_fns.append(f_dve_mask)
                    else:
                        msk_fns.append(None)
                    c_msk_w = ctr["c_msk2"]

                    def f_pe_pv(pe, w=w, kj=kj, gA=gA, gB=gB, diag=diag,
                                c_exp_w=c_exp_w, c_msk_w=c_msk_w, first=(kj == 0),
                                last=(kj == nkj - 1), d_att_pair=d_att_pair):
                        if diag:
                            pe.wait_ge(sems["c_msk2"], c_msk_w)
                        else:
                            pe.wait_ge(sems["c_exp"], c_exp_w)
                        if first:
                            if isinstance(d_att_pair, tuple):
                                pe.wait_ge(sems["d_att"], d_att_pair[0])
                                pe.wait_ge(sems["a_ev"], d_att_pair[1])
                            elif d_att_pair > 0:
                                pe.wait_ge(sems["d_att"], d_att_pair)
                        pb = probs0 if w % 2 == 0 else probs1
                        pe.matmul(psCA[0:65, :],
                                  vp[:, 2080 * gA + 65 * kj:2080 * gA + 65 * kj + 65],
                                  pb[:, 0:512], start=first, stop=last)
                        pe.matmul(psCB[0:65, :],
                                  vp[:, 2080 * gB + 65 * kj:2080 * gB + 65 * kj + 65],
                                  pb[:, 512:1024], start=first,
                                  stop=last).then_inc(sems["c_pv"], 1)
                    bump("c_pv")
                    pv_fns.append(f_pe_pv)
                    pv_at_wave.append(ctr["c_pv"])
                    wave += 1
                sc_pair_snap.append(ctr["c_sc"])

                em("tensor", sc_fns[0])
                if nkj > 1:
                    em("tensor", sc_fns[1])
                for kj in range(nkj):
                    em("scalar", exp_fns[kj])
                    if msk_fns[kj] is not None:
                        em("vector", msk_fns[kj])
                    em("tensor", pv_fns[kj])
                    if kj + 2 < nkj:
                        em("tensor", sc_fns[kj + 2])
                    if pend_lazy:
                        em(*pend_lazy.pop(0))
                while pend_lazy:
                    em(*pend_lazy.pop(0))

                c_pv_last = ctr["c_pv"]

                def f_dve_ctxevac(v, c_pv_last=c_pv_last,
                                  d_prev_norm=d_norm_prev[0]):
                    v.wait_ge(sems["c_pv"], c_pv_last)
                    if d_prev_norm > 0:
                        v.wait_ge(sems["d_att"], d_prev_norm)
                    SE(v, v.tensor_copy(ctxn[:, 0:512], psCA[0:64, 0:512]))
                    v.tensor_copy(ctxn[:, 512:1024],
                                  psCB[0:64, 0:512]).then_inc(sems["d_att"], 1)
                bump("d_att")
                em("vector", f_dve_ctxevac)
                d_evac = ctr["d_att"]

                def f_act_denevac(s, c_pv_last=c_pv_last,
                                  d_prev_norm=d_norm_prev[0]):
                    s.wait_ge(sems["c_pv"], c_pv_last)
                    if d_prev_norm > 0:
                        s.wait_ge(sems["d_att"], d_prev_norm)
                    SEA(s, s.activation(recip[64:65, 0:512], psCA[64:65, 0:512],
                                        AF.Identity))
                    s.activation(ktmp[64:65, 512:1024], psCB[64:65, 0:512],
                                 AF.Identity).then_inc(sems["a_ev"], 1)
                bump("a_ev")
                em("scalar", f_act_denevac)
                a_ev_p = ctr["a_ev"]
                d_evac_prev[0] = (d_evac, a_ev_p)

                for hh in range(2):
                    p_prev = ctr["p_att"]
                    d_prev_chain = ctr["d_att"]

                    if sg < 4 or p == 7:
                        def f_act_rcp(s, hh=hh, a_ev_p=a_ev_p, p_prev=p_prev,
                                      d_prev_chain=d_prev_chain):
                            s.wait_ge(sems["a_ev"], a_ev_p)
                            if hh == 0:
                                src_row = recip[64:65, 0:512]
                            else:
                                s.wait_ge(sems["d_att"], d_prev_chain)
                                src_row = ktmp[64:65, 512:1024]
                            SEA(s, s.activation(ktmp[64:65, 0:512], src_row, AF.Ln))
                            s.activation(recip[64:65, 0:512], ktmp[64:65, 0:512],
                                         AF.Exp, scale=-1.0).then_inc(sems["a_rc"], 1)
                        bump("a_rc")
                        pend_lazy.append(("scalar", f_act_rcp))
                        a_rc_h = ctr["a_rc"]

                        def f_dve_hilo(v, a_rc_h=a_rc_h):
                            v.wait_ge(sems["a_rc"], a_rc_h)
                            SE(v, v.tensor_copy(rhil[64:65, 0:512], recip[64:65, 0:512]))
                            v.tensor_tensor(rhil[64:65, 512:1024], recip[64:65, 0:512],
                                            rhil[64:65, 0:512],
                                            ALU.subtract).then_inc(sems["d_att"], 1)
                        bump("d_att")
                        pend_lazy.append(("vector", f_dve_hilo))
                    else:
                        def f_dve_rcp(v, hh=hh, a_ev_p=a_ev_p, p_prev=p_prev,
                                      d_prev_chain=d_prev_chain):
                            v.wait_ge(sems["a_ev"], a_ev_p)
                            if hh == 0:
                                src_row = recip[64:65, 0:512]
                            else:
                                v.wait_ge(sems["d_att"], d_prev_chain)
                                src_row = ktmp[64:65, 512:1024]
                            SE(v, v.reciprocal(recip[64:65, 0:512], src_row))
                            SE(v, v.tensor_copy(rhil[64:65, 0:512], recip[64:65, 0:512]))
                            v.tensor_tensor(rhil[64:65, 512:1024], recip[64:65, 0:512],
                                            rhil[64:65, 0:512],
                                            ALU.subtract).then_inc(sems["d_att"], 1)
                        bump("d_att")
                        pend_lazy.append(("vector", f_dve_rcp))
                    d_rcp_h = ctr["d_att"]

                    def f_pe_bc(pe, d_rcp_h=d_rcp_h):
                        pe.wait_ge(sems["d_att"], d_rcp_h)
                        pe.matmul(psM[0:64, 0:512], onesr[64:65, 0:64],
                                  rhil[64:65, 0:512], start=True, stop=False)
                        pe.matmul(psM[0:64, 0:512], onesr[64:65, 0:64],
                                  rhil[64:65, 512:1024], start=False,
                                  stop=True).then_inc(sems["p_att"], 1)
                    bump("p_att")
                    pend_lazy.append(("tensor", f_pe_bc))
                    p_bc = ctr["p_att"]

                    def f_dve_norm(v, hh=hh, p_bc=p_bc):
                        v.wait_ge(sems["p_att"], p_bc)
                        v.tensor_tensor(ctxn[:, 512 * hh:512 * hh + 512],
                                        ctxn[:, 512 * hh:512 * hh + 512],
                                        psM[0:64, 0:512],
                                        ALU.mult).then_inc(sems["d_att"], 1)
                    bump("d_att")
                    pend_lazy.append(("vector", f_dve_norm))
                d_norm = ctr["d_att"]
                d_norm_prev[0] = d_norm
                first_pair = (pair_gl == 0)

                def f_pe_asm(pe, d_norm=d_norm, first_pair=first_pair):
                    if first_pair:
                        pe.wait_ge(sems["s_sm2"], SSM_CONST)
                    pe.wait_ge(sems["d_att"], d_norm)
                    pe.matmul(psM[:, 0:512], ident[0:64, :], ctxn[:, 0:512],
                              start=True, stop=False)
                    pe.matmul(psM[:, 0:512], asmb[0:64, :], ctxn[:, 512:1024],
                              start=False, stop=True).then_inc(sems["p_att"], 1)
                bump("p_att")
                pend_lazy.append(("tensor", f_pe_asm))
                p_asm = ctr["p_att"]

                def f_dve_asme(v, p=p, p_asm=p_asm):
                    v.wait_ge(sems["p_att"], p_asm)
                    v.tensor_copy(ctxf[:, 512 * p:512 * p + 512],
                                  psM[:, 0:512]).then_inc(sems["d_att"], 1)
                bump("d_att")
                pend_lazy.append(("vector", f_dve_asme))
                asme_vals.append(ctr["d_att"])
                pair_gl += 1

            d_ctxf = ctr["d_att"]
            exp_all = ctr["c_exp"]
            c_comb_sg0 = ctr["c_comb"]
            n7 = 10 if sg < 4 else 8
            while len(pend_lazy) > n7:
                em(*pend_lazy.pop(0))

            def mk_wo_head(u, d_p6=asme_vals[6], exp_all=exp_all):
                def f_pe_wo_head(pe, u=u, d_p6=d_p6, exp_all=exp_all):
                    ps = PS[u % 2]
                    if u == 0:
                        pe.wait_ge(sems["s_wt_o"], SWT_WO)
                    pe.wait_ge(sems["c_exp"], exp_all)
                    pe.wait_ge(sems["d_att"], d_p6)
                    for p in range(7):
                        for hh in range(2):
                            pe.matmul(ps[:, 512 * hh:512 * hh + 512],
                                      ctxf[:, 512 * p + 128 * u:512 * p + 128 * u + 128],
                                      woT[:, 1024 * p + 512 * hh:1024 * p + 512 * hh + 512],
                                      start=(p == 0), stop=False)
                return f_pe_wo_head

            em("tensor", mk_wo_head(0))
            for _ in range(n7 // 2 - 1):
                em(*pend_lazy.pop(0))
            em("tensor", mk_wo_head(1))
            while pend_lazy:
                em(*pend_lazy.pop(0))
            d_ctxf = ctr["d_att"]
            for u in range(4):
                ps = PS[u % 2]

                def f_pe_wo(pe, u=u, ps=ps, d_ctxf=d_ctxf, exp_all=exp_all,
                            c_comb_sg0=c_comb_sg0):
                    if u < 2:
                        pe.wait_ge(sems["d_att"], d_ctxf)
                        for hh in range(2):
                            inst = pe.matmul(ps[:, 512 * hh:512 * hh + 512],
                                      ctxf[:, 512 * 7 + 128 * u:512 * 7 + 128 * u + 128],
                                      woT[:, 1024 * 7 + 512 * hh:1024 * 7 + 512 * hh + 512],
                                      start=False, stop=True)
                    else:
                        pe.wait_ge(sems["c_comb"], c_comb_sg0 + u - 1)
                        for hh in range(2):
                            for p in range(8):
                                inst = pe.matmul(ps[:, 512 * hh:512 * hh + 512],
                                          ctxf[:, 512 * p + 128 * u:512 * p + 128 * u + 128],
                                          woT[:, 1024 * p + 512 * hh:1024 * p + 512 * hh + 512],
                                          start=(p == 0), stop=(p == 7))
                    inst.then_inc(sems["p_att"], 1)
                bump("p_att")
                em("tensor", f_pe_wo)
                p_wo = ctr["p_att"]
                ob = ffnb if u % 2 == 0 else ffnb2

                def f_dve_comb(v, sg=sg, u=u, ps=ps, ob=ob, p_wo=p_wo):
                    v.wait_ge(sems["p_att"], p_wo)
                    v.wait_ge(sems["s_pl%d" % u], 16 * (sg + 1))
                    cu = 4 * sg + u
                    if cu >= 2:
                        v.wait_ge(sems["s_out%d" % (cu % 2)], 16 * (cu // 2))
                    v.tensor_tensor(ob[:, :], ps[:, :],
                                    xstage[:, 1024 * u:1024 * u + 1024],
                                    ALU.add).then_inc(sems["c_comb"], 1)
                bump("c_comb")
                em("vector", f_dve_comb)
                c_comb_u = ctr["c_comb"]

                def f_gp_ostore(gp, sg=sg, u=u, ob=ob, c_comb_u=c_comb_u):
                    gp.wait_ge(sems["c_comb"], c_comb_u)
                    gp.dma_start(out_d[128 * (4 * sg + u):128 * (4 * sg + u) + 128, :],
                                 ob[:, :]).then_inc(sems["s_out%d" % ((4 * sg + u) % 2)], 16)
                bump("s_out%d" % ((4 * sg + u) % 2), 16)
                em("gpsimd", f_gp_ostore)

        def f_gp_fin(gp):
            if PH2:
                gp.wait_ge(sems["s_out0"], 16 * 16)
                gp.wait_ge(sems["s_out1"], 16 * 16)
            else:
                gp.wait_ge(sems["s_p0"], 16 * 16)
                gp.wait_ge(sems["s_p1"], 16 * 16)
                gp.wait_ge(sems["s_q0"], 32 * 32)
                gp.wait_ge(sems["s_q1"], 32 * 32)
                gp.wait_ge(sems["s_wt_o"], SWT_WO)
        em("gpsimd", f_gp_fin)

        blk = nc.Block()
        blk.__enter__()
        blk.gpsimd(setup_gpsimd)
        blk.sync(setup_sync)
        blk.vector(setup_vector)
        blk.__exit__(None, None, None)

        blk = nc.Block()
        blk.__enter__()
        for eng in ("sync", "tensor", "vector", "scalar", "gpsimd"):
            fns = [fn for e, fn in prog if e == eng]

            def section(engine, fns=fns):
                for fn in fns:
                    fn(engine)

            getattr(blk, eng)(section)
        blk.__exit__(None, None, None)

    return nc


_NC_CACHE = [None]


def _consts():
    t = np.arange(128).reshape(128, 1)
    u = np.arange(896).reshape(1, 896)
    masks = (t <= u - 384).astype(ml_dtypes.bfloat16)
    ident = np.eye(128, dtype=ml_dtypes.bfloat16)
    asmb = np.zeros((128, 128), dtype=ml_dtypes.bfloat16)
    for k in range(64):
        asmb[k, 64 + k] = 1
    return masks, ident, asmb


def kernel(L0_attn_x, L0_attn_wq, L0_attn_wk, L0_attn_wv, L0_attn_wo,
           L0_ffn_W_r, L0_ffn_W_k, L0_ffn_W_v, L0_ffn_mu_r, L0_ffn_mu_k):
    x = np.ascontiguousarray(np.asarray(L0_attn_x, dtype=np.float32))
    masks, ident, asmb = _consts()
    shared = dict(
        wq=np.ascontiguousarray(np.asarray(L0_attn_wq, np.float32)),
        wk=np.ascontiguousarray(np.asarray(L0_attn_wk, np.float32)),
        wv=np.ascontiguousarray(np.asarray(L0_attn_wv, np.float32)),
        wo=np.ascontiguousarray(np.asarray(L0_attn_wo, np.float32)),
        wr=np.ascontiguousarray(np.asarray(L0_ffn_W_r, np.float32)),
        wkk=np.ascontiguousarray(np.asarray(L0_ffn_W_k, np.float32)),
        wvv=np.ascontiguousarray(np.asarray(L0_ffn_W_v, np.float32)),
        mur=np.ascontiguousarray(np.asarray(L0_ffn_mu_r, np.float32)),
        muk=np.ascontiguousarray(np.asarray(L0_ffn_mu_k, np.float32)),
        masks=masks, ident=ident, asmb=asmb,
    )
    if _NC_CACHE[0] is None:
        _NC_CACHE[0] = build_nc()
    nc = _NC_CACHE[0]
    in_maps = [dict(shared, x=np.ascontiguousarray(x[b])) for b in range(8)]
    res = run_bass_kernel_spmd(nc, in_maps, core_ids=list(range(8)))
    out = np.stack([np.asarray(r["out"], dtype=np.float32) for r in res.results], axis=0)
    return out


# version 31
# speedup vs baseline: 1.1786x; 1.1786x over previous
import contextlib
import numpy as np
import ml_dtypes
import concourse.bass as bass
import concourse.mybir as mybir
from concourse.bass_utils import run_bass_kernel_spmd

F32 = mybir.dt.float32
BF16 = mybir.dt.bfloat16
AF = mybir.ActivationFunctionType
ALU = mybir.AluOpType
AX = mybir.AxisListType

S = 4096
C = 1024
NG = 8
DK = 256
EPS = 1e-5
BUILD_PHASE = 2

SWT_WQ = 16 * 8
SWT_WKV = 16 * 16
SWT_WF = 16 * 24
SWT_WO = 16 * 8
SSM_MU = 16 * 16
SSM_CONST = 16 * 3


def build_nc():
    nc = bass.Bass("TRN2", target_bir_lowering=False)

    x_d = nc.dram_tensor("x", [S, C], F32, kind="ExternalInput").ap()
    wq_d = nc.dram_tensor("wq", [C, C], F32, kind="ExternalInput").ap()
    wk_d = nc.dram_tensor("wk", [DK, C], F32, kind="ExternalInput").ap()
    wv_d = nc.dram_tensor("wv", [DK, C], F32, kind="ExternalInput").ap()
    wo_d = nc.dram_tensor("wo", [C, C], F32, kind="ExternalInput").ap()
    wr_d = nc.dram_tensor("wr", [C, C], F32, kind="ExternalInput").ap()
    wkk_d = nc.dram_tensor("wkk", [C, C], F32, kind="ExternalInput").ap()
    wvv_d = nc.dram_tensor("wvv", [C, C], F32, kind="ExternalInput").ap()
    mur_d = nc.dram_tensor("mur", [C], F32, kind="ExternalInput").ap()
    muk_d = nc.dram_tensor("muk", [C], F32, kind="ExternalInput").ap()
    masks_d = nc.dram_tensor("masks", [128, 896], BF16, kind="ExternalInput").ap()
    ident_d = nc.dram_tensor("ident", [128, 128], BF16, kind="ExternalInput").ap()
    asmb_d = nc.dram_tensor("asmb", [128, 128], BF16, kind="ExternalInput").ap()
    out_d = nc.dram_tensor("out", [S, C], F32, kind="ExternalOutput").ap()

    whq = nc.dram_tensor("whq", [C, C], BF16).ap()
    whk = nc.dram_tensor("whk", [DK, C], BF16).ap()
    whv = nc.dram_tensor("whv", [DK, C], BF16).ap()
    who = nc.dram_tensor("who", [C, C], BF16).ap()
    whr = nc.dram_tensor("whr", [C, C], BF16).ap()
    whkk = nc.dram_tensor("whkk", [C, C], BF16).ap()
    whvv = nc.dram_tensor("whvv", [C, C], BF16).ap()
    qscr = nc.dram_tensor("qscr", [8, 128, S], BF16).ap()

    with contextlib.ExitStack() as es:
        E = es.enter_context
        wqT = E(nc.sbuf_tensor("wqT", [128, 8192], BF16))
        wkT = E(nc.sbuf_tensor("wkT", [128, 2048], BF16))
        wvT = E(nc.sbuf_tensor("wvT", [128, 2048], BF16))
        woT = E(nc.sbuf_tensor("woT", [128, 8192], BF16))
        wrT = E(nc.sbuf_tensor("wrT", [128, 8192], BF16))
        wkkT = E(nc.sbuf_tensor("wkkT", [128, 8192], BF16))
        wvvT = E(nc.sbuf_tensor("wvvT", [128, 8192], BF16))
        kf = E(nc.sbuf_tensor("kf", [128, 8192], BF16))
        vp = E(nc.sbuf_tensor("vp", [128, 8320], BF16))
        qsl = E(nc.sbuf_tensor("qsl", [128, 4096], BF16))
        xnw = E(nc.sbuf_tensor("xnw", [128, 4104], BF16))
        masks = E(nc.sbuf_tensor("masks_sb", [128, 896], BF16))
        ident = E(nc.sbuf_tensor("ident_sb", [128, 128], BF16))
        asmb = E(nc.sbuf_tensor("asmb_sb", [128, 128], BF16))
        onesr = E(nc.sbuf_tensor("onesr", [128, 128], BF16))
        xstage = E(nc.sbuf_tensor("xstage", [128, 5120], F32))
        qstage = E(nc.sbuf_tensor("qstage", [128, 1024], BF16))
        xr = E(nc.sbuf_tensor("xr", [128, 2048], BF16))
        xk = E(nc.sbuf_tensor("xk", [128, 2048], BF16))
        kact = E(nc.sbuf_tensor("kact", [128, 2048], BF16))
        mixt = E(nc.sbuf_tensor("mixt", [128, 256], BF16))
        ktmp = E(nc.sbuf_tensor("ktmp", [128, 1024], F32))
        rtok = E(nc.sbuf_tensor("rtok", [128, 1024], F32))
        ffnb = E(nc.sbuf_tensor("ffnb", [128, 1024], F32))
        ffnb2 = E(nc.sbuf_tensor("ffnb2", [128, 1024], F32))
        probs01 = E(nc.sbuf_tensor("probs01", [128, 2048], BF16))
        probs0 = probs01[:, 0:1024]
        probs1 = probs01[:, 1024:2048]
        qsqs = [probs0[:, 0:512], probs0[:, 512:1024]]
        xntok = probs1[:, :]
        ctxf = E(nc.sbuf_tensor("ctxf", [128, 4096], BF16))
        ctxn = E(nc.sbuf_tensor("ctxn", [64, 1024], BF16))
        recip = E(nc.sbuf_tensor("recip", [128, 512], F32))
        rhil = E(nc.sbuf_tensor("rhil", [128, 1024], BF16))
        stt = E(nc.sbuf_tensor("stt", [128, 8], F32))
        sumx = E(nc.sbuf_tensor("sumx", [128, 1], F32))
        sumsq = E(nc.sbuf_tensor("sumsq", [128, 1], F32))
        meanc = E(nc.sbuf_tensor("meanc", [128, 1], F32))
        m2c = E(nc.sbuf_tensor("m2c", [128, 1], F32))
        varc = E(nc.sbuf_tensor("varc", [128, 1], F32))
        lnv = E(nc.sbuf_tensor("lnv", [128, 1], F32))
        rstd = E(nc.sbuf_tensor("rstd", [128, 1], F32))
        nmr = E(nc.sbuf_tensor("nmr", [128, 1], F32))
        epsc = E(nc.sbuf_tensor("epsc", [128, 1], F32))
        negc = E(nc.sbuf_tensor("negc", [128, 1], F32))
        negc1 = E(nc.sbuf_tensor("negc1", [128, 1], BF16))
        murc = E(nc.sbuf_tensor("murc", [128, 8], F32))
        mukc = E(nc.sbuf_tensor("mukc", [128, 8], F32))

        psS01 = E(nc.psum_tensor("psS01", [128, 2048], F32))
        psS0 = psS01[:, 0:1024]
        psS1 = psS01[:, 1024:2048]
        psCA = E(nc.psum_tensor("psCA", [128, 512], F32))
        psCB = E(nc.psum_tensor("psCB", [128, 512], F32))
        psM = E(nc.psum_tensor("psM", [128, 512], F32))
        psT = E(nc.psum_tensor("psT", [128, 1024], BF16))

        sem_names = [
            "s_x0", "s_x1", "s_x2", "s_x3", "s_x4", "s_w", "s_wf", "s_wo2",
            "s_wt_q", "s_wt_kv", "s_wt_f", "s_wt_o",
            "s_sm", "s_sm2", "s_q0", "s_q1", "s_qs", "s_p0", "s_p1",
            "s_pl0", "s_pl1", "s_pl2", "s_pl3", "s_out0", "s_out1",
            "a_ln", "d_ln", "c_tp", "c_tpe", "c_proj", "c_pevac", "c_qkv",
            "c_sm", "c_smr", "c_mix", "c_rp", "a_sig", "c_kp", "c_krelu",
            "c_vp", "c_ffnd", "c_sc", "c_exp", "c_msk2", "c_pv",
            "d_att", "p_att", "c_comb", "c_negc", "a_rc", "a_ev",
        ]
        sem_names.append("d_self")
        sem_names.append("a_self")
        sems = {n: E(nc.semaphore(n)) for n in sem_names}
        ctr = {n: 0 for n in sem_names}
        prog = []
        _dself = {"n": 0}
        _aself = {"n": 0}

        def SE(v, inst):
            _dself["n"] += 1
            inst.then_inc(sems["d_self"], 1)
            v.wait_ge(sems["d_self"], _dself["n"])

        def SEA(s, inst):
            _aself["n"] += 1
            inst.then_inc(sems["a_self"], 1)
            s.wait_ge(sems["a_self"], _aself["n"])

        def em(engine, fn):
            prog.append((engine, fn))

        def bump(name, n=1):
            ctr[name] += n
            return ctr[name]

        def pair_of(h):
            a, r = h // 8, h % 8
            return 4 * a + (r % 4), r // 4

        PS = [psS0, psS1]
        XNW = lambda: xnw[:, :].rearrange("p (c w) -> p c w", c=8)

        def setup_gpsimd(g):
            g.dma_start(whq[:, :], wq_d[:, :]).then_inc(sems["s_w"], 16)
            g.dma_start(whk[:, :], wk_d[:, :]).then_inc(sems["s_w"], 16)
            g.dma_start(whv[:, :], wv_d[:, :]).then_inc(sems["s_w"], 16)
            g.dma_start(whr[:, :], wr_d[:, :]).then_inc(sems["s_wf"], 16)
            g.dma_start(whkk[:, :], wkk_d[:, :]).then_inc(sems["s_wf"], 16)
            g.dma_start(whvv[:, :], wvv_d[:, :]).then_inc(sems["s_wf"], 16)
            for b in range(16):
                p, slot = b // 2, b % 2
                a, i = p // 4, p % 4
                h = 8 * a + 4 * slot + i
                g.dma_start(who[:, 64 * b:64 * b + 64],
                            wo_d[:, 64 * h:64 * h + 64]).then_inc(sems["s_wo2"], 16)
            for j in range(8):
                g.dma_start(murc[:, j:j + 1],
                            mur_d[128 * j:128 * j + 128].rearrange(
                                "(p one) -> p one", p=128, one=1)).then_inc(sems["s_sm"], 16)
                g.dma_start(mukc[:, j:j + 1],
                            muk_d[128 * j:128 * j + 128].rearrange(
                                "(p one) -> p one", p=128, one=1)).then_inc(sems["s_sm"], 16)

        def setup_sync(sy):
            sy.dma_start(masks[:, :], masks_d[:, :]).then_inc(sems["s_sm2"], 16)
            sy.dma_start(ident[:, :], ident_d[:, :]).then_inc(sems["s_sm2"], 16)
            sy.dma_start(asmb[:, :], asmb_d[:, :]).then_inc(sems["s_sm2"], 16)

        def setup_vector(v):
            v.memset(onesr[:, :], 1.0)
            v.memset(epsc[:, :], EPS)
            v.memset(stt[:, :], 0.0)
            v.memset(vp[:, :].rearrange("p (g k w) -> p g k w", g=4, k=32)[:, :, :, 64:65], 1.0)
            v.memset(XNW()[:, :, 0:1], 0.0)

        def emit_xf(src, dst, nchunk, rows, semn):
            for cc in range(nchunk):
                def fn(sy, src=src, dst=dst, cc=cc, rows=rows, semn=semn):
                    sy.dma_start(dst[:, rows * cc:rows * cc + rows],
                                 src[:, 128 * cc:128 * cc + 128],
                                 transpose=True).then_inc(sems[semn], 16)
                bump(semn, 16)
                em("sync", fn)

        def emit_x(t):
            def fn(sy, t=t):
                if t >= 5:
                    sy.wait_ge(sems["c_ffnd"], t - 4)
                sy.dma_start(xstage[:, 1024 * (t % 5):1024 * (t % 5) + 1024],
                             x_d[128 * t:128 * t + 128, :]).then_inc(sems["s_x%d" % (t % 5)], 16)
            bump("s_x%d" % (t % 5), 16)
            em("sync", fn)

        def emit_wait_sw(thr):
            em("sync", lambda sy, thr=thr: sy.wait_ge(sems["s_w"], thr))

        for t in range(4):
            emit_x(t)
        emit_wait_sw(16 * 3)
        emit_xf(whq, wqT, 8, 1024, "s_wt_q")
        emit_x(4)
        emit_xf(whk, wkT, 8, 256, "s_wt_kv")
        emit_xf(whv, wvT, 8, 256, "s_wt_kv")
        em("sync", lambda sy: sy.wait_ge(sems["s_wf"], 16 * 3))
        emit_xf(whr, wrT, 8, 1024, "s_wt_f")
        emit_xf(whkk, wkkT, 8, 1024, "s_wt_f")
        emit_xf(whvv, wvvT, 8, 1024, "s_wt_f")
        em("sync", lambda sy: sy.wait_ge(sems["s_wo2"], 16 * 16))
        emit_xf(who, woT, 8, 1024, "s_wt_o")
        for t in range(5, 32):
            emit_x(t)

        qevac_cnt = []
        qk_state = {"n": 0, "sm_vals": []}
        pend_stat = []

        def build_ln(g):
            carry = None
            if g >= 1:
                def f_dve_carry(v):
                    SE(v, v.tensor_copy(XNW()[:, :, 0:1], XNW()[:, :, 512:513]))
                carry = ("vector", f_dve_carry)
            tchains = []
            for j in range(4):
                t = 4 * g + j
                tj = []

                c_sm_snap = ctr["c_sm"]

                def f_act_sq(s, t=t, c_sm_snap=c_sm_snap):
                    s.wait_ge(sems["s_x%d" % (t % 5)], 16 * (t // 5 + 1))
                    if c_sm_snap > 0:
                        s.wait_ge(sems["c_sm"], c_sm_snap)
                    if t >= 1:
                        s.wait_ge(sems["d_ln"], 2 * t - 1)
                    s.activation(probs0[:, :], xstage[:, 1024 * (t % 5):1024 * (t % 5) + 1024],
                                 AF.Square, accum_out=sumsq[:, :]).then_inc(sems["a_ln"], 1)
                bump("a_ln")
                tj.append(("scalar", f_act_sq))
                a_sq = ctr["a_ln"]

                def f_dve_stats(v, t=t, a_sq=a_sq):
                    v.wait_ge(sems["s_x%d" % (t % 5)], 16 * (t // 5 + 1))
                    if t >= 1:
                        v.wait_ge(sems["a_ln"], a_sq - 1)
                    SE(v, v.reduce_sum(sumx[:, :],
                                       xstage[:, 1024 * (t % 5):1024 * (t % 5) + 1024],
                                       axis=AX.X))
                    SE(v, v.tensor_scalar(meanc[:, :], sumx[:, :], 1.0 / C, None, ALU.mult))
                    SE(v, v.tensor_tensor(m2c[:, :], meanc[:, :], meanc[:, :], ALU.mult))
                    v.wait_ge(sems["a_ln"], a_sq)
                    v.scalar_tensor_tensor(varc[:, :], sumsq[:, :], 1.0 / C, m2c[:, :],
                                           ALU.mult, ALU.subtract).then_inc(sems["d_ln"], 1)
                bump("d_ln")
                tj.append(("vector", f_dve_stats))
                d_var = ctr["d_ln"]

                def f_act_rstd(s, d_var=d_var):
                    s.wait_ge(sems["d_ln"], d_var)
                    SEA(s, s.activation(lnv[:, :], varc[:, :], AF.Ln, bias=epsc[:, :]))
                    s.activation(rstd[:, :], lnv[:, :], AF.Exp,
                                 scale=-0.5).then_inc(sems["a_ln"], 1)
                bump("a_ln")
                tj.append(("scalar", f_act_rstd))
                a_rstd = ctr["a_ln"]

                def f_dve_nmr(v, a_rstd=a_rstd):
                    v.wait_ge(sems["a_ln"], a_rstd)
                    v.scalar_tensor_tensor(nmr[:, :], meanc[:, :], -1.0, rstd[:, :],
                                           ALU.mult, ALU.mult).then_inc(sems["d_ln"], 1)
                bump("d_ln")
                tj.append(("vector", f_dve_nmr))
                d_nmr = ctr["d_ln"]

                def f_act_apply(s, t=t, d_nmr=d_nmr):
                    s.wait_ge(sems["d_ln"], d_nmr)
                    if t >= 1:
                        s.wait_ge(sems["c_tp"], t)
                    s.activation(xntok[:, :], xstage[:, 1024 * (t % 5):1024 * (t % 5) + 1024],
                                 AF.Identity, bias=nmr[:, :],
                                 scale=rstd[:, :]).then_inc(sems["a_ln"], 1)
                bump("a_ln")
                tj.append(("scalar", f_act_apply))
                a_app = ctr["a_ln"]

                def f_pe_tp(pe, t=t, a_app=a_app):
                    pe.wait_ge(sems["a_ln"], a_app)
                    pe.wait_ge(sems["s_sm2"], SSM_CONST)
                    if t >= 1:
                        pe.wait_ge(sems["c_tpe"], t)
                    for cc in range(8):
                        inst = pe.transpose(psT[:, 128 * cc:128 * cc + 128],
                                            xntok[:, 128 * cc:128 * cc + 128], ident[:, :])
                    inst.then_inc(sems["c_tp"], 1)
                bump("c_tp")
                tj.append(("tensor", f_pe_tp))
                c_tp_t = ctr["c_tp"]

                def f_dve_evac(v, g=g, j=j, c_tp_t=c_tp_t):
                    v.wait_ge(sems["c_tp"], c_tp_t)
                    if j == 0 and g >= 1:
                        v.wait_ge(sems["c_qkv"], g)
                    dst = XNW()[:, :, 1 + 128 * j:1 + 128 * j + 128]
                    src = psT[:, :].rearrange("p (c w) -> p c w", c=8)
                    v.tensor_copy(dst, src).then_inc(sems["c_tpe"], 1)
                bump("c_tpe")
                tj.append(("vector", f_dve_evac))
                tchains.append(tj)
            return carry, tchains

        ln0_carry, ln0_t = build_ln(0)
        for tj in ln0_t:
            for it in tj:
                em(*it)

        for g in range(NG):
            krelu_grp_snap = ctr["c_krelu"]
            c_tpe_g = ctr["c_tpe"]

            mix0_base = ctr["c_mix"]
            mix0_items = []
            c_rp_s0 = ctr["c_rp"]
            c_kp_s0 = ctr["c_kp"]
            for cc in range(8):
                def f_dve_mix0(v, cc=cc, c_tpe_g=c_tpe_g, c_rp_s0=c_rp_s0,
                               c_kp_s0=c_kp_s0, mix0_base=mix0_base):
                    if cc == 0:
                        v.wait_ge(sems["c_tpe"], c_tpe_g)
                        v.wait_ge(sems["s_sm"], SSM_MU)
                        if c_rp_s0 > 0:
                            v.wait_ge(sems["c_rp"], c_rp_s0)
                        if c_kp_s0 > 0:
                            v.wait_ge(sems["c_kp"], c_kp_s0)
                    else:
                        v.wait_ge(sems["c_mix"], mix0_base + 2 * cc)
                    cur = XNW()[:, cc, 1:257]
                    prv = XNW()[:, cc, 0:256]
                    SE(v, v.tensor_tensor(mixt[:, :], cur, prv, ALU.subtract))
                    v.scalar_tensor_tensor(xr[:, 256 * cc:256 * cc + 256], mixt[:, :],
                                           murc[:, cc:cc + 1], prv,
                                           ALU.mult, ALU.add).then_inc(sems["c_mix"], 1)
                    v.scalar_tensor_tensor(xk[:, 256 * cc:256 * cc + 256], mixt[:, :],
                                           mukc[:, cc:cc + 1], prv,
                                           ALU.mult, ALU.add).then_inc(sems["c_mix"], 1)
                bump("c_mix", 2)
                mix0_items.append(("vector", f_dve_mix0))
            mix0_h = ctr["c_mix"]

            group_units = [("q", j) for j in range(8)] + \
                          [("k", a) for a in range(2)] + \
                          [("v", u) for u in range(4)]

            for ui, (kind, idx) in enumerate(group_units):
                unit_no = len(qevac_cnt)
                quarters = [(psS0, 0), (psS0, 512), (psS1, 0), (psS1, 512),
                            (psCA, 0), (psCB, 0)]
                half, hoff = quarters[unit_no % 6]

                def f_pe_proj(pe, g=g, kind=kind, idx=idx, half=half, hoff=hoff,
                              unit_no=unit_no, c_tpe_g=c_tpe_g, ui=ui,
                              krelu_grp_snap=krelu_grp_snap):
                    if unit_no == 0:
                        pe.wait_ge(sems["s_wt_q"], SWT_WQ)
                    if unit_no == 8:
                        pe.wait_ge(sems["s_wt_kv"], SWT_WKV)
                    pe.wait_ge(sems["c_tpe"], c_tpe_g)
                    if ui < 6 and g >= 1:
                        pe.wait_ge(sems["c_ffnd"], 4 * g)
                        pe.wait_ge(sems["c_krelu"], krelu_grp_snap)
                    if unit_no >= 6:
                        pe.wait_ge(sems["c_pevac"], qevac_cnt[unit_no - 6])
                    if kind == "q":
                        for cc in range(8):
                            inst = pe.matmul(half[:, hoff:hoff + 512],
                                      wqT[:, 1024 * cc + 128 * idx:1024 * cc + 128 * idx + 128],
                                      XNW()[:, cc, 1:513],
                                      start=(cc == 0), stop=(cc == 7))
                    elif kind == "k":
                        for cc in range(8):
                            inst = pe.matmul(half[:, hoff:hoff + 512],
                                      wkT[:, 256 * cc + 128 * idx:256 * cc + 128 * idx + 128],
                                      XNW()[:, cc, 1:513],
                                      start=(cc == 0), stop=(cc == 7))
                    else:
                        for cc in range(8):
                            inst = pe.matmul(half[:, hoff:hoff + 256],
                                      XNW()[:, cc, 1 + 128 * idx:1 + 128 * idx + 128],
                                      wvT[:, 256 * cc:256 * cc + 256],
                                      start=(cc == 0), stop=(cc == 7))
                    inst.then_inc(sems["c_proj"], 1)
                bump("c_proj")
                em("tensor", f_pe_proj)
                c_proj_u = ctr["c_proj"]

                if kind in ("q", "k"):
                    jg = 8 * g + idx if kind == "q" else None
                    qk_no = qk_state["n"]
                    qk_state["n"] += 1
                    qsq = qsqs[qk_no % 2]
                    sm_prev = (qk_state["sm_vals"][qk_no - 2]
                               if qk_no >= 2 else 0)
                    c_pev_u = ctr["c_pevac"] + 2

                    def f_dve_pevac(v, g=g, kind=kind, idx=idx, half=half, hoff=hoff,
                                    c_proj_u=c_proj_u, jg=jg, sm_prev=sm_prev,
                                    c_pev_u=c_pev_u, qsq=qsq):
                        v.wait_ge(sems["c_proj"], c_proj_u)
                        if sm_prev > 0:
                            v.wait_ge(sems["c_sm"], sm_prev)
                        if kind == "q":
                            if jg >= 2:
                                v.wait_ge(sems["s_q%d" % (jg % 2)], 32 * (jg // 2))
                            q_half = qstage[:, 512 * (jg % 2):512 * (jg % 2) + 512]
                            v.tensor_scalar(q_half, half[:, hoff:hoff + 512],
                                            0.125, None,
                                            ALU.mult).then_inc(sems["c_pevac"], 1)
                            v.wait_ge(sems["c_pevac"], c_pev_u - 1)
                            v.tensor_tensor(qsq[:, :], q_half, q_half,
                                            ALU.mult).then_inc(sems["c_pevac"], 1)
                        else:
                            kdst = kf[:, 4096 * idx + 512 * g:4096 * idx + 512 * g + 512]
                            v.tensor_copy(kdst,
                                          half[:, hoff:hoff + 512]).then_inc(sems["c_pevac"], 1)
                            v.wait_ge(sems["c_pevac"], c_pev_u - 1)
                            v.tensor_tensor(qsq[:, :], kdst, kdst,
                                            ALU.mult).then_inc(sems["c_pevac"], 1)
                    bump("c_pevac", 2)
                    em("vector", f_dve_pevac)
                    assert c_pev_u == ctr["c_pevac"]
                    c_smr_snap = ctr["c_smr"]
                    c_smr_fin = ctr["c_smr"] + 2

                    def f_pe_stat(pe, c_pev_u=c_pev_u, c_smr_snap=c_smr_snap, qsq=qsq):
                        pe.wait_ge(sems["c_pevac"], c_pev_u)
                        pe.wait_ge(sems["c_smr"], c_smr_snap)
                        pe.matmul(psM[0:1, 0:512], onesr[:, 0:1], qsq[:, :],
                                  start=True, stop=True).then_inc(sems["c_sm"], 1)
                    bump("c_sm")
                    qk_state["sm_vals"].append(ctr["c_sm"])
                    c_sm_u = ctr["c_sm"]
                    col = 0 if kind == "q" else 1

                    def f_dve_stat(v, c_sm_u=c_sm_u, col=col, c_smr_fin=c_smr_fin):
                        v.wait_ge(sems["c_sm"], c_sm_u)
                        v.reduce_max(stt[0:1, 2:3], psM[0:1, 0:512],
                                     axis=AX.X).then_inc(sems["c_smr"], 1)
                        v.wait_ge(sems["c_smr"], c_smr_fin - 1)
                        v.tensor_tensor(stt[0:1, col:col + 1], stt[0:1, col:col + 1],
                                        stt[0:1, 2:3], ALU.max).then_inc(sems["c_smr"], 1)
                        v.wait_ge(sems["c_smr"], c_smr_fin)
                    bump("c_smr", 2)
                    while pend_stat:
                        em(*pend_stat.pop(0))
                    pend_stat.append(("tensor", f_pe_stat))
                    pend_stat.append(("vector", f_dve_stat))

                    if kind == "q":
                        def f_gp_qstore(gp, g=g, idx=idx, jg=jg, c_pev_u=c_pev_u):
                            gp.wait_ge(sems["c_pevac"], c_pev_u)
                            for hh, h in enumerate((2 * idx, 2 * idx + 1)):
                                p, slot = pair_of(h)
                                gp.dma_start(
                                    qscr[p, 64 * slot:64 * slot + 64, 512 * g:512 * g + 512],
                                    qstage[64 * hh:64 * hh + 64,
                                           512 * (jg % 2):512 * (jg % 2) + 512]
                                ).then_inc(sems["s_q%d" % (jg % 2)], 16)
                        bump("s_q%d" % (jg % 2), 32)
                        em("gpsimd", f_gp_qstore)
                else:
                    def f_dve_vevac(v, g=g, idx=idx, half=half, hoff=hoff, c_proj_u=c_proj_u):
                        v.wait_ge(sems["c_proj"], c_proj_u)
                        kj = 4 * g + idx
                        dst = vp[:, :].rearrange("p (gg k w) -> p gg k w",
                                                 gg=4, k=32)[:, :, kj, 0:64]
                        src = half[:, hoff:hoff + 256].rearrange("p (gg w) -> p gg w", gg=4)
                        v.tensor_copy(dst, src).then_inc(sems["c_pevac"], 1)
                    bump("c_pevac")
                    em("vector", f_dve_vevac)
                qevac_cnt.append(ctr["c_pevac"])
                if mix0_items:
                    em(*mix0_items.pop(0))

            grp_evac_all = ctr["c_pevac"]
            while mix0_items:
                em(*mix0_items.pop(0))
            while pend_stat:
                em(*pend_stat.pop(0))

            def f_pe_qkv_done(pe):
                pe.nop().then_inc(sems["c_qkv"], 1)
            bump("c_qkv")
            em("tensor", f_pe_qkv_done)

            ln_next = build_ln(g + 1) if g + 1 < NG else None
            mix1_base = ctr["c_mix"]
            mix1_items = []
            c_rp_s1 = ctr["c_rp"] + 2
            c_kp_s1 = ctr["c_kp"] + 2
            for cc in range(8):
                def f_dve_mix1(v, cc=cc, c_tpe_g=c_tpe_g, c_rp_s1=c_rp_s1,
                               c_kp_s1=c_kp_s1, mix1_base=mix1_base):
                    if cc == 0:
                        v.wait_ge(sems["c_tpe"], c_tpe_g)
                        v.wait_ge(sems["s_sm"], SSM_MU)
                        v.wait_ge(sems["c_rp"], c_rp_s1)
                        v.wait_ge(sems["c_kp"], c_kp_s1)
                    else:
                        v.wait_ge(sems["c_mix"], mix1_base + 2 * cc)
                    cur = XNW()[:, cc, 257:513]
                    prv = XNW()[:, cc, 256:512]
                    SE(v, v.tensor_tensor(mixt[:, :], cur, prv, ALU.subtract))
                    v.scalar_tensor_tensor(xr[:, 256 * cc:256 * cc + 256], mixt[:, :],
                                           murc[:, cc:cc + 1], prv,
                                           ALU.mult, ALU.add).then_inc(sems["c_mix"], 1)
                    v.scalar_tensor_tensor(xk[:, 256 * cc:256 * cc + 256], mixt[:, :],
                                           mukc[:, cc:cc + 1], prv,
                                           ALU.mult, ALU.add).then_inc(sems["c_mix"], 1)
                bump("c_mix", 2)
                mix1_items.append(("vector", f_dve_mix1))
            mix1_h = ctr["c_mix"]
            for hg in range(2):
                spliced = ln_next is not None and hg == 1
                LT = ln_next[1] if ln_next is not None else None
                c_rp_snap = ctr["c_rp"]
                c_kp_snap = ctr["c_kp"]
                if hg == 0:
                    c_mix_h = mix0_h
                    c_vp_half_snap = ctr["c_vp"]
                else:
                    while mix1_items:
                        em(*mix1_items.pop(0))
                    if ln_next is not None and ln_next[0] is not None:
                        em(*ln_next[0])
                    c_mix_h = mix1_h
                    c_vp_half_snap = ctr["c_vp"]

                for kw in range(2):
                    c_krelu_snap = ctr["c_krelu"]

                    def f_pe_k(pe, g=g, hg=hg, kw=kw, c_mix_h=c_mix_h,
                               c_krelu_snap=c_krelu_snap, grp_evac_all=grp_evac_all):
                        if g == 0 and hg == 0 and kw == 0:
                            pe.wait_ge(sems["s_wt_f"], SWT_WF)
                        if kw == 0:
                            pe.wait_ge(sems["c_pevac"], grp_evac_all)
                        if c_krelu_snap > 0:
                            pe.wait_ge(sems["c_krelu"], c_krelu_snap - 1)
                        ktiles = [psS1[:, 0:256], psS1[:, 512:768],
                                  psCA[:, 0:256], psCB[:, 0:256]]
                        for cc in range(8):
                            pe.wait_ge(sems["c_mix"], c_mix_h - 16 + 2 * cc + 2)
                            for dk4 in range(4):
                                dk = 4 * kw + dk4
                                inst = pe.matmul(ktiles[dk4],
                                          wkkT[:, 1024 * cc + 128 * dk:1024 * cc + 128 * dk + 128],
                                          xk[:, 256 * cc:256 * cc + 256],
                                          start=(cc == 0), stop=(cc == 7))
                        inst.then_inc(sems["c_kp"], 1)
                    bump("c_kp")
                    if spliced:
                        em(*LT[kw][0])
                        em(*LT[kw][2])
                    em("tensor", f_pe_k)
                    c_kp_w = ctr["c_kp"]

                    krelu_fin = ctr["c_krelu"] + 5

                    def f_dve_krelu(v, kw=kw, c_kp_w=c_kp_w, c_vp_half_snap=c_vp_half_snap,
                                    krelu_fin=krelu_fin):
                        v.wait_ge(sems["c_kp"], c_kp_w)
                        if krelu_fin > 5:
                            v.wait_ge(sems["c_krelu"], krelu_fin - 5)
                        if c_vp_half_snap > 0:
                            v.wait_ge(sems["c_vp"], c_vp_half_snap)
                        ktiles = [psS1[:, 0:256], psS1[:, 512:768],
                                  psCA[:, 0:256], psCB[:, 0:256]]
                        for qd in range(4):
                            v.tensor_scalar(ktmp[:, 256 * qd:256 * qd + 256], ktiles[qd],
                                            0.0, None,
                                            ALU.max).then_inc(sems["c_krelu"], 1)
                        v.wait_ge(sems["c_krelu"], krelu_fin - 1)
                        v.tensor_tensor(kact[:, 1024 * kw:1024 * kw + 1024],
                                        ktmp[:, :], ktmp[:, :],
                                        ALU.mult).then_inc(sems["c_krelu"], 1)
                    bump("c_krelu", 5)
                    em("vector", f_dve_krelu)
                    if spliced:
                        em(*LT[kw][1])
                        em(*LT[kw][3])
                        em(*LT[kw][4])
                if spliced:
                    em(*LT[0][5])
                c_krelu_h = ctr["c_krelu"]

                for mm in range(2):
                    m = 2 * hg + mm
                    mi = 4 * g + m

                    def f_pe_r(pe, g=g, m=m, mm=mm, mi=mi, c_mix_h=c_mix_h,
                               grp_evac_all=grp_evac_all):
                        if g == 0 and m == 0:
                            pe.wait_ge(sems["s_wt_f"], SWT_WF)
                        if m == 0:
                            pe.wait_ge(sems["c_pevac"], grp_evac_all)
                        if mi >= 1:
                            pe.wait_ge(sems["c_ffnd"], mi)
                        for cc in range(8):
                            pe.wait_ge(sems["c_mix"], c_mix_h - 16 + 2 * cc + 1)
                            for hh in range(2):
                                inst = pe.matmul(psS0[:, 512 * hh:512 * hh + 512],
                                          xr[:, 256 * cc + 128 * mm:256 * cc + 128 * mm + 128],
                                          wrT[:, 1024 * cc + 512 * hh:1024 * cc + 512 * hh + 512],
                                          start=(cc == 0), stop=(cc == 7))
                        inst.then_inc(sems["c_rp"], 1)
                    bump("c_rp")
                    em("tensor", f_pe_r)
                    if spliced and mm == 0:
                        em(*LT[1][5])
                        em(*LT[0][6])
                    c_rp_m = ctr["c_rp"]

                    def f_act_sig(s, c_rp_m=c_rp_m, a1=ctr["a_sig"] + 1):
                        s.wait_ge(sems["c_rp"], c_rp_m)
                        s.activation(rtok[:, :], psS0[:, :], AF.Exp,
                                     scale=-1.0).then_inc(sems["a_sig"], 1)
                        s.wait_ge(sems["a_sig"], a1)
                        SEA(s, s.activation(rtok[:, :], rtok[:, :], AF.Ln,
                                            bias=1.0))
                        s.activation(rtok[:, :], rtok[:, :], AF.Exp,
                                     scale=-1.0).then_inc(sems["a_sig"], 1)
                    bump("a_sig", 2)
                    em("scalar", f_act_sig)
                    if hg == 0 and mm == 1 and mix1_items:
                        em(*mix1_items.pop(0))
                    a_sig_m = ctr["a_sig"] - 1
                    if spliced:
                        if mm == 0:
                            em(*LT[1][6])
                        tn = 2 + mm
                        for it in (LT[tn][0], LT[tn][2], LT[tn][1], LT[tn][3],
                                   LT[tn][4]):
                            em(*it)
                    a_sig2_m = ctr["a_sig"]

                    def f_pe_vout(pe, g=g, mi=mi, mm=mm, c_krelu_h=c_krelu_h,
                                  a_sig_m=a_sig_m):
                        if g == 0 and mi == 0:
                            pe.wait_ge(sems["s_wt_f"], SWT_WF)
                        pe.wait_ge(sems["c_krelu"], c_krelu_h)
                        pe.wait_ge(sems["a_sig"], a_sig_m)
                        for hh in range(2):
                            for dk in range(8):
                                inst = pe.matmul(psS0[:, 512 * hh:512 * hh + 512],
                                          kact[:, 256 * dk + 128 * mm:256 * dk + 128 * mm + 128],
                                          wvvT[:, 1024 * dk + 512 * hh:1024 * dk + 512 * hh + 512],
                                          start=(dk == 0), stop=(dk == 7))
                        inst.then_inc(sems["c_vp"], 1)
                    bump("c_vp")
                    em("tensor", f_pe_vout)
                    if spliced:
                        em(*LT[2 + mm][5])
                    c_vp_m = ctr["c_vp"]
                    fb = ffnb if mi % 2 == 0 else ffnb2

                    def f_dve_ffn(v, mi=mi, fb=fb, c_vp_m=c_vp_m, a_sig2_m=a_sig2_m):
                        v.wait_ge(sems["c_vp"], c_vp_m)
                        v.wait_ge(sems["a_sig"], a_sig2_m)
                        if mi >= 2:
                            v.wait_ge(sems["s_p%d" % (mi % 2)], 16 * (mi // 2))
                        SE(v, v.tensor_tensor(fb[:, :], psS0[:, :], rtok[:, :], ALU.mult))
                        v.tensor_tensor(fb[:, :], fb[:, :],
                                        xstage[:, 1024 * (mi % 5):1024 * (mi % 5) + 1024],
                                        ALU.add).then_inc(sems["c_ffnd"], 1)
                    bump("c_ffnd")
                    em("vector", f_dve_ffn)
                    c_ffnd_m = ctr["c_ffnd"]

                    def f_gp_pstore(gp, mi=mi, fb=fb, c_ffnd_m=c_ffnd_m):
                        gp.wait_ge(sems["c_ffnd"], c_ffnd_m)
                        gp.dma_start(out_d[128 * mi:128 * mi + 128, :],
                                     fb[:, :]).then_inc(sems["s_p%d" % (mi % 2)], 16)
                    bump("s_p%d" % (mi % 2), 16)
                    em("gpsimd", f_gp_pstore)
                    if hg == 0 and mm == 1:
                        while mix1_items:
                            em(*mix1_items.pop(0))
                    if spliced:
                        em(*LT[2 + mm][6])

        def f_dve_cprep(v):
            v.tensor_tensor(stt[0:1, 3:4], stt[0:1, 0:1], stt[0:1, 1:2],
                            ALU.mult).then_inc(sems["d_att"], 1)
        bump("d_att")
        em("vector", f_dve_cprep)
        d_cp = ctr["d_att"]

        def f_act_c(s, d_cp=d_cp):
            s.wait_ge(sems["d_att"], d_cp)
            SEA(s, s.activation(stt[0:1, 4:5], stt[0:1, 3:4], AF.Ln))
            s.activation(stt[0:1, 5:6], stt[0:1, 4:5], AF.Exp,
                         scale=0.5).then_inc(sems["a_ln"], 1)
        bump("a_ln")
        em("scalar", f_act_c)
        a_c = ctr["a_ln"]

        def f_dve_negc1(v, a_c=a_c):
            v.wait_ge(sems["a_ln"], a_c)
            v.tensor_scalar(negc1[0:1, 0:1], stt[0:1, 5:6], -1.0, -0.5,
                            ALU.mult, ALU.add).then_inc(sems["d_att"], 1)
        bump("d_att")
        em("vector", f_dve_negc1)
        d_nc1 = ctr["d_att"]

        def f_pe_negc(pe, d_nc1=d_nc1):
            pe.wait_ge(sems["d_att"], d_nc1)
            pe.matmul(psM[0:128, 0:1], onesr[0:1, :], negc1[0:1, 0:1],
                      start=True, stop=True).then_inc(sems["p_att"], 1)
        bump("p_att")
        em("tensor", f_pe_negc)
        p_nc = ctr["p_att"]

        def f_dve_negc(v, p_nc=p_nc):
            v.wait_ge(sems["p_att"], p_nc)
            v.tensor_copy(negc[:, :], psM[:, 0:1]).then_inc(sems["c_negc"], 1)
        bump("c_negc")
        em("vector", f_dve_negc)

        PH2 = BUILD_PHASE >= 2
        wave = 0
        pair_gl = 0
        pend_lazy = []
        d_evac_prev = [0]
        d_norm_prev = [0]
        exp_at_wave = []
        pv_at_wave = []

        pend_epi = []
        sc_pair_snap = []
        for sg in range(8 if PH2 else 0):
            for pr in range(8):
                qs_prev = ctr["s_qs"]

                def f_sy_qsl(sy, sg=sg, pr=pr, qs_prev=qs_prev,
                             sc_snap=(sc_pair_snap[8 * (sg - 1) + pr] if sg >= 1 else 0)):
                    if sg == 0 and pr == 0:
                        sy.wait_ge(sems["s_q0"], 32 * 32)
                        sy.wait_ge(sems["s_q1"], 32 * 32)
                    if qs_prev > 0:
                        sy.wait_ge(sems["s_qs"], qs_prev)
                    if sg >= 1:
                        sy.wait_ge(sems["c_sc"], sc_snap)
                    sy.dma_start(
                        qsl[:, 512 * pr:512 * pr + 512],
                        qscr[pr:pr + 1, :, 512 * sg:512 * sg + 512].rearrange(
                            "p r c -> r (p c)")).then_inc(sems["s_qs"], 16)
                bump("s_qs", 16)
                em("sync", f_sy_qsl)
            s_qs_sg = ctr["s_qs"]

            for u in range(4):
                def f_sy_pl(sy, sg=sg, u=u):
                    if sg == 0 and u == 0:
                        sy.wait_ge(sems["s_p0"], 16 * 16)
                        sy.wait_ge(sems["s_p1"], 16 * 16)
                        sy.wait_ge(sems["c_ffnd"], 32)
                    if sg >= 1:
                        sy.wait_ge(sems["c_comb"], 4 * (sg - 1) + u + 1)
                    sy.dma_start(xstage[:, 1024 * u:1024 * u + 1024],
                                 out_d[128 * (4 * sg + u):128 * (4 * sg + u) + 128, :]
                                 ).then_inc(sems["s_pl%d" % u], 16)
                bump("s_pl%d" % u, 16)
                em("sync", f_sy_pl)

            asme_vals = []
            for p in range(8):
                a = p // 4
                gA, gB = 2 * a, 2 * a + 1
                nkj = 4 * sg + 4
                d_att_pair = d_evac_prev[0]
                sc_fns = []
                pv_fns = []
                exp_fns = []
                msk_fns = []
                for kj in range(nkj):
                    w = wave
                    pi = w % 2
                    diag = kj >= 4 * sg
                    dd = 128 * kj - 512 * sg if diag else 0

                    def f_pe_sc(pe, w=w, pi=pi, p=p, a=a, kj=kj, sg=sg,
                                s_qs_sg=s_qs_sg):
                        if w == 0:
                            pe.wait_ge(sems["c_negc"], 1)
                        pe.wait_ge(sems["s_qs"], 16 * (8 * sg + p + 1))
                        if w >= 2:
                            pe.wait_ge(sems["c_exp"], exp_at_wave[w - 2])
                        if p == 0 and kj <= 1 and sg >= 1:
                            pe.wait_ge(sems["c_comb"], 4 * sg)
                        ps = PS[pi]
                        pe.matmul(ps[:, 0:512],
                                  kf[0:64, 4096 * a + 128 * kj:4096 * a + 128 * kj + 128],
                                  qsl[0:64, 512 * p:512 * p + 512], start=True, stop=True)
                        pe.matmul(ps[:, 512:1024],
                                  kf[64:128, 4096 * a + 128 * kj:4096 * a + 128 * kj + 128],
                                  qsl[64:128, 512 * p:512 * p + 512],
                                  start=True, stop=True).then_inc(sems["c_sc"], 1)
                    bump("c_sc")
                    sc_fns.append(f_pe_sc)
                    c_sc_w = ctr["c_sc"]

                    def f_act_exp(s, w=w, pi=pi, c_sc_w=c_sc_w):
                        if w == 0:
                            s.wait_ge(sems["c_negc"], 1)
                        s.wait_ge(sems["c_sc"], c_sc_w)
                        if w >= 2:
                            s.wait_ge(sems["c_pv"], pv_at_wave[w - 2])
                        pb = probs0 if w % 2 == 0 else probs1
                        s.activation(pb[:, :], PS[pi][:, :], AF.Exp,
                                     bias=negc[:, :]).then_inc(sems["c_exp"], 1)
                    bump("c_exp")
                    exp_fns.append(f_act_exp)
                    exp_at_wave.append(ctr["c_exp"])
                    c_exp_w = ctr["c_exp"]

                    if diag:
                        def f_dve_mask(v, w=w, dd=dd, c_exp_w=c_exp_w):
                            v.wait_ge(sems["c_exp"], c_exp_w)
                            if w < 8:
                                v.wait_ge(sems["s_sm2"], SSM_CONST)
                            pb = probs0 if w % 2 == 0 else probs1
                            mv = masks[:, 384 - dd:384 - dd + 512]
                            v.tensor_tensor(pb[:, 0:512], pb[:, 0:512], mv,
                                            ALU.mult).then_inc(sems["c_msk2"], 1)
                            v.tensor_tensor(pb[:, 512:1024], pb[:, 512:1024], mv,
                                            ALU.mult).then_inc(sems["c_msk2"], 1)
                        bump("c_msk2", 2)
                        msk_fns.append(f_dve_mask)
                    else:
                        msk_fns.append(None)
                    c_msk_w = ctr["c_msk2"]

                    def f_pe_pv(pe, w=w, kj=kj, gA=gA, gB=gB, diag=diag,
                                c_exp_w=c_exp_w, c_msk_w=c_msk_w, first=(kj == 0),
                                last=(kj == nkj - 1), d_att_pair=d_att_pair):
                        if diag:
                            pe.wait_ge(sems["c_msk2"], c_msk_w)
                        else:
                            pe.wait_ge(sems["c_exp"], c_exp_w)
                        if first:
                            if isinstance(d_att_pair, tuple):
                                pe.wait_ge(sems["d_att"], d_att_pair[0])
                                pe.wait_ge(sems["a_ev"], d_att_pair[1])
                            elif d_att_pair > 0:
                                pe.wait_ge(sems["d_att"], d_att_pair)
                        pb = probs0 if w % 2 == 0 else probs1
                        pe.matmul(psCA[0:65, :],
                                  vp[:, 2080 * gA + 65 * kj:2080 * gA + 65 * kj + 65],
                                  pb[:, 0:512], start=first, stop=last)
                        pe.matmul(psCB[0:65, :],
                                  vp[:, 2080 * gB + 65 * kj:2080 * gB + 65 * kj + 65],
                                  pb[:, 512:1024], start=first,
                                  stop=last).then_inc(sems["c_pv"], 1)
                    bump("c_pv")
                    pv_fns.append(f_pe_pv)
                    pv_at_wave.append(ctr["c_pv"])
                    wave += 1
                sc_pair_snap.append(ctr["c_sc"])

                em("tensor", sc_fns[0])
                if nkj > 1:
                    em("tensor", sc_fns[1])
                for kj in range(nkj):
                    em("scalar", exp_fns[kj])
                    if msk_fns[kj] is not None:
                        em("vector", msk_fns[kj])
                    em("tensor", pv_fns[kj])
                    if kj + 2 < nkj:
                        em("tensor", sc_fns[kj + 2])
                    if pend_lazy:
                        em(*pend_lazy.pop(0))
                while pend_lazy:
                    em(*pend_lazy.pop(0))

                c_pv_last = ctr["c_pv"]

                def f_dve_ctxevac(v, c_pv_last=c_pv_last,
                                  d_prev_norm=d_norm_prev[0]):
                    v.wait_ge(sems["c_pv"], c_pv_last)
                    if d_prev_norm > 0:
                        v.wait_ge(sems["d_att"], d_prev_norm)
                    SE(v, v.tensor_copy(ctxn[:, 0:512], psCA[0:64, 0:512]))
                    v.tensor_copy(ctxn[:, 512:1024],
                                  psCB[0:64, 0:512]).then_inc(sems["d_att"], 1)
                bump("d_att")
                em("vector", f_dve_ctxevac)
                d_evac = ctr["d_att"]

                def f_act_denevac(s, c_pv_last=c_pv_last,
                                  d_prev_norm=d_norm_prev[0]):
                    s.wait_ge(sems["c_pv"], c_pv_last)
                    if d_prev_norm > 0:
                        s.wait_ge(sems["d_att"], d_prev_norm)
                    SEA(s, s.activation(recip[64:65, 0:512], psCA[64:65, 0:512],
                                        AF.Identity))
                    s.activation(ktmp[64:65, 512:1024], psCB[64:65, 0:512],
                                 AF.Identity).then_inc(sems["a_ev"], 1)
                bump("a_ev")
                em("scalar", f_act_denevac)
                a_ev_p = ctr["a_ev"]
                d_evac_prev[0] = (d_evac, a_ev_p)

                for hh in range(2):
                    p_prev = ctr["p_att"]
                    d_prev_chain = ctr["d_att"]

                    if sg < 4 or p == 7:
                        def f_act_rcp(s, hh=hh, a_ev_p=a_ev_p, p_prev=p_prev,
                                      d_prev_chain=d_prev_chain):
                            s.wait_ge(sems["a_ev"], a_ev_p)
                            if hh == 0:
                                src_row = recip[64:65, 0:512]
                            else:
                                s.wait_ge(sems["d_att"], d_prev_chain)
                                src_row = ktmp[64:65, 512:1024]
                            SEA(s, s.activation(ktmp[64:65, 0:512], src_row, AF.Ln))
                            s.activation(recip[64:65, 0:512], ktmp[64:65, 0:512],
                                         AF.Exp, scale=-1.0).then_inc(sems["a_rc"], 1)
                        bump("a_rc")
                        pend_lazy.append(("scalar", f_act_rcp))
                        a_rc_h = ctr["a_rc"]

                        def f_dve_hilo(v, a_rc_h=a_rc_h):
                            v.wait_ge(sems["a_rc"], a_rc_h)
                            SE(v, v.tensor_copy(rhil[64:65, 0:512], recip[64:65, 0:512]))
                            v.tensor_tensor(rhil[64:65, 512:1024], recip[64:65, 0:512],
                                            rhil[64:65, 0:512],
                                            ALU.subtract).then_inc(sems["d_att"], 1)
                        bump("d_att")
                        pend_lazy.append(("vector", f_dve_hilo))
                    else:
                        def f_dve_rcp(v, hh=hh, a_ev_p=a_ev_p, p_prev=p_prev,
                                      d_prev_chain=d_prev_chain):
                            v.wait_ge(sems["a_ev"], a_ev_p)
                            if hh == 0:
                                src_row = recip[64:65, 0:512]
                            else:
                                v.wait_ge(sems["d_att"], d_prev_chain)
                                src_row = ktmp[64:65, 512:1024]
                            SE(v, v.reciprocal(recip[64:65, 0:512], src_row))
                            SE(v, v.tensor_copy(rhil[64:65, 0:512], recip[64:65, 0:512]))
                            v.tensor_tensor(rhil[64:65, 512:1024], recip[64:65, 0:512],
                                            rhil[64:65, 0:512],
                                            ALU.subtract).then_inc(sems["d_att"], 1)
                        bump("d_att")
                        pend_lazy.append(("vector", f_dve_rcp))
                    d_rcp_h = ctr["d_att"]

                    def f_pe_bc(pe, d_rcp_h=d_rcp_h):
                        pe.wait_ge(sems["d_att"], d_rcp_h)
                        pe.matmul(psM[0:64, 0:512], onesr[64:65, 0:64],
                                  rhil[64:65, 0:512], start=True, stop=False)
                        pe.matmul(psM[0:64, 0:512], onesr[64:65, 0:64],
                                  rhil[64:65, 512:1024], start=False,
                                  stop=True).then_inc(sems["p_att"], 1)
                    bump("p_att")
                    pend_lazy.append(("tensor", f_pe_bc))
                    p_bc = ctr["p_att"]

                    def f_dve_norm(v, hh=hh, p_bc=p_bc):
                        v.wait_ge(sems["p_att"], p_bc)
                        v.tensor_tensor(ctxn[:, 512 * hh:512 * hh + 512],
                                        ctxn[:, 512 * hh:512 * hh + 512],
                                        psM[0:64, 0:512],
                                        ALU.mult).then_inc(sems["d_att"], 1)
                    bump("d_att")
                    pend_lazy.append(("vector", f_dve_norm))
                d_norm = ctr["d_att"]
                d_norm_prev[0] = d_norm
                first_pair = (pair_gl == 0)

                def f_pe_asm(pe, d_norm=d_norm, first_pair=first_pair):
                    if first_pair:
                        pe.wait_ge(sems["s_sm2"], SSM_CONST)
                    pe.wait_ge(sems["d_att"], d_norm)
                    pe.matmul(psM[:, 0:512], ident[0:64, :], ctxn[:, 0:512],
                              start=True, stop=False)
                    pe.matmul(psM[:, 0:512], asmb[0:64, :], ctxn[:, 512:1024],
                              start=False, stop=True).then_inc(sems["p_att"], 1)
                bump("p_att")
                pend_lazy.append(("tensor", f_pe_asm))
                p_asm = ctr["p_att"]

                def f_dve_asme(v, p=p, p_asm=p_asm):
                    v.wait_ge(sems["p_att"], p_asm)
                    v.tensor_copy(ctxf[:, 512 * p:512 * p + 512],
                                  psM[:, 0:512]).then_inc(sems["d_att"], 1)
                bump("d_att")
                pend_lazy.append(("vector", f_dve_asme))
                asme_vals.append(ctr["d_att"])
                pair_gl += 1

            d_ctxf = ctr["d_att"]
            exp_all = ctr["c_exp"]
            c_comb_sg0 = ctr["c_comb"]
            n7 = 10 if sg < 4 else 8
            while len(pend_lazy) > n7:
                em(*pend_lazy.pop(0))

            def mk_wo_head(u, d_p6=asme_vals[6], exp_all=exp_all):
                def f_pe_wo_head(pe, u=u, d_p6=d_p6, exp_all=exp_all):
                    ps = PS[u % 2]
                    if u == 0:
                        pe.wait_ge(sems["s_wt_o"], SWT_WO)
                    pe.wait_ge(sems["c_exp"], exp_all)
                    pe.wait_ge(sems["d_att"], d_p6)
                    for p in range(7):
                        for hh in range(2):
                            pe.matmul(ps[:, 512 * hh:512 * hh + 512],
                                      ctxf[:, 512 * p + 128 * u:512 * p + 128 * u + 128],
                                      woT[:, 1024 * p + 512 * hh:1024 * p + 512 * hh + 512],
                                      start=(p == 0), stop=False)
                return f_pe_wo_head

            em("tensor", mk_wo_head(0))
            for _ in range(n7 // 2 - 1):
                em(*pend_lazy.pop(0))
            em("tensor", mk_wo_head(1))
            while pend_lazy:
                em(*pend_lazy.pop(0))
            d_ctxf = ctr["d_att"]
            for u in range(4):
                ps = PS[u % 2]

                def f_pe_wo(pe, u=u, ps=ps, d_ctxf=d_ctxf, exp_all=exp_all,
                            c_comb_sg0=c_comb_sg0):
                    if u < 2:
                        pe.wait_ge(sems["d_att"], d_ctxf)
                        for hh in range(2):
                            inst = pe.matmul(ps[:, 512 * hh:512 * hh + 512],
                                      ctxf[:, 512 * 7 + 128 * u:512 * 7 + 128 * u + 128],
                                      woT[:, 1024 * 7 + 512 * hh:1024 * 7 + 512 * hh + 512],
                                      start=False, stop=True)
                    else:
                        pe.wait_ge(sems["c_comb"], c_comb_sg0 + u - 1)
                        for hh in range(2):
                            for p in range(8):
                                inst = pe.matmul(ps[:, 512 * hh:512 * hh + 512],
                                          ctxf[:, 512 * p + 128 * u:512 * p + 128 * u + 128],
                                          woT[:, 1024 * p + 512 * hh:1024 * p + 512 * hh + 512],
                                          start=(p == 0), stop=(p == 7))
                    inst.then_inc(sems["p_att"], 1)
                bump("p_att")
                em("tensor", f_pe_wo)
                p_wo = ctr["p_att"]
                ob = ffnb if u % 2 == 0 else ffnb2

                def f_dve_comb(v, sg=sg, u=u, ps=ps, ob=ob, p_wo=p_wo):
                    v.wait_ge(sems["p_att"], p_wo)
                    v.wait_ge(sems["s_pl%d" % u], 16 * (sg + 1))
                    cu = 4 * sg + u
                    if cu >= 2:
                        v.wait_ge(sems["s_out%d" % (cu % 2)], 16 * (cu // 2))
                    v.tensor_tensor(ob[:, :], ps[:, :],
                                    xstage[:, 1024 * u:1024 * u + 1024],
                                    ALU.add).then_inc(sems["c_comb"], 1)
                bump("c_comb")
                em("vector", f_dve_comb)
                c_comb_u = ctr["c_comb"]

                def f_gp_ostore(gp, sg=sg, u=u, ob=ob, c_comb_u=c_comb_u):
                    gp.wait_ge(sems["c_comb"], c_comb_u)
                    gp.dma_start(out_d[128 * (4 * sg + u):128 * (4 * sg + u) + 128, :],
                                 ob[:, :]).then_inc(sems["s_out%d" % ((4 * sg + u) % 2)], 16)
                bump("s_out%d" % ((4 * sg + u) % 2), 16)
                em("gpsimd", f_gp_ostore)

        def f_gp_fin(gp):
            if PH2:
                gp.wait_ge(sems["s_out0"], 16 * 16)
                gp.wait_ge(sems["s_out1"], 16 * 16)
            else:
                gp.wait_ge(sems["s_p0"], 16 * 16)
                gp.wait_ge(sems["s_p1"], 16 * 16)
                gp.wait_ge(sems["s_q0"], 32 * 32)
                gp.wait_ge(sems["s_q1"], 32 * 32)
                gp.wait_ge(sems["s_wt_o"], SWT_WO)
        em("gpsimd", f_gp_fin)

        blk = nc.Block()
        blk.__enter__()
        blk.gpsimd(setup_gpsimd)
        blk.sync(setup_sync)
        blk.vector(setup_vector)
        blk.__exit__(None, None, None)

        blk = nc.Block()
        blk.__enter__()
        for eng in ("sync", "tensor", "vector", "scalar", "gpsimd"):
            fns = [fn for e, fn in prog if e == eng]

            def section(engine, fns=fns):
                for fn in fns:
                    fn(engine)

            getattr(blk, eng)(section)
        blk.__exit__(None, None, None)

    return nc


_NC_CACHE = [None]


def _consts():
    t = np.arange(128).reshape(128, 1)
    u = np.arange(896).reshape(1, 896)
    masks = (t <= u - 384).astype(ml_dtypes.bfloat16)
    ident = np.eye(128, dtype=ml_dtypes.bfloat16)
    asmb = np.zeros((128, 128), dtype=ml_dtypes.bfloat16)
    for k in range(64):
        asmb[k, 64 + k] = 1
    return masks, ident, asmb


def kernel(L0_attn_x, L0_attn_wq, L0_attn_wk, L0_attn_wv, L0_attn_wo,
           L0_ffn_W_r, L0_ffn_W_k, L0_ffn_W_v, L0_ffn_mu_r, L0_ffn_mu_k):
    x = np.ascontiguousarray(np.asarray(L0_attn_x, dtype=np.float32))
    masks, ident, asmb = _consts()
    shared = dict(
        wq=np.ascontiguousarray(np.asarray(L0_attn_wq, np.float32)),
        wk=np.ascontiguousarray(np.asarray(L0_attn_wk, np.float32)),
        wv=np.ascontiguousarray(np.asarray(L0_attn_wv, np.float32)),
        wo=np.ascontiguousarray(np.asarray(L0_attn_wo, np.float32)),
        wr=np.ascontiguousarray(np.asarray(L0_ffn_W_r, np.float32)),
        wkk=np.ascontiguousarray(np.asarray(L0_ffn_W_k, np.float32)),
        wvv=np.ascontiguousarray(np.asarray(L0_ffn_W_v, np.float32)),
        mur=np.ascontiguousarray(np.asarray(L0_ffn_mu_r, np.float32)),
        muk=np.ascontiguousarray(np.asarray(L0_ffn_mu_k, np.float32)),
        masks=masks, ident=ident, asmb=asmb,
    )
    if _NC_CACHE[0] is None:
        _NC_CACHE[0] = build_nc()
    nc = _NC_CACHE[0]
    in_maps = [dict(shared, x=np.ascontiguousarray(x[b])) for b in range(8)]
    res = run_bass_kernel_spmd(nc, in_maps, core_ids=list(range(8)))
    out = np.stack([np.asarray(r["out"], dtype=np.float32) for r in res.results], axis=0)
    return out


# version 32
# speedup vs baseline: 1.1950x; 1.0139x over previous
import contextlib
import numpy as np
import ml_dtypes
import concourse.bass as bass
import concourse.mybir as mybir
from concourse.bass_utils import run_bass_kernel_spmd

F32 = mybir.dt.float32
BF16 = mybir.dt.bfloat16
AF = mybir.ActivationFunctionType
ALU = mybir.AluOpType
AX = mybir.AxisListType

S = 4096
C = 1024
NG = 8
DK = 256
EPS = 1e-5
BUILD_PHASE = 2

SWT_WQ = 16 * 8
SWT_WKV = 16 * 16
SWT_WF = 16 * 24
SWT_WO = 16 * 8
SSM_MU = 16 * 16
SSM_CONST = 16 * 3


def build_nc():
    nc = bass.Bass("TRN2", target_bir_lowering=False)

    x_d = nc.dram_tensor("x", [S, C], F32, kind="ExternalInput").ap()
    wq_d = nc.dram_tensor("wq", [C, C], F32, kind="ExternalInput").ap()
    wk_d = nc.dram_tensor("wk", [DK, C], F32, kind="ExternalInput").ap()
    wv_d = nc.dram_tensor("wv", [DK, C], F32, kind="ExternalInput").ap()
    wo_d = nc.dram_tensor("wo", [C, C], F32, kind="ExternalInput").ap()
    wr_d = nc.dram_tensor("wr", [C, C], F32, kind="ExternalInput").ap()
    wkk_d = nc.dram_tensor("wkk", [C, C], F32, kind="ExternalInput").ap()
    wvv_d = nc.dram_tensor("wvv", [C, C], F32, kind="ExternalInput").ap()
    mur_d = nc.dram_tensor("mur", [C], F32, kind="ExternalInput").ap()
    muk_d = nc.dram_tensor("muk", [C], F32, kind="ExternalInput").ap()
    masks_d = nc.dram_tensor("masks", [128, 896], BF16, kind="ExternalInput").ap()
    ident_d = nc.dram_tensor("ident", [128, 128], BF16, kind="ExternalInput").ap()
    asmb_d = nc.dram_tensor("asmb", [128, 128], BF16, kind="ExternalInput").ap()
    out_d = nc.dram_tensor("out", [S, C], F32, kind="ExternalOutput").ap()

    whq = nc.dram_tensor("whq", [C, C], BF16).ap()
    whk = nc.dram_tensor("whk", [DK, C], BF16).ap()
    whv = nc.dram_tensor("whv", [DK, C], BF16).ap()
    who = nc.dram_tensor("who", [C, C], BF16).ap()
    whr = nc.dram_tensor("whr", [C, C], BF16).ap()
    whkk = nc.dram_tensor("whkk", [C, C], BF16).ap()
    whvv = nc.dram_tensor("whvv", [C, C], BF16).ap()
    qscr = nc.dram_tensor("qscr", [8, 128, S], BF16).ap()

    with contextlib.ExitStack() as es:
        E = es.enter_context
        wqT = E(nc.sbuf_tensor("wqT", [128, 8192], BF16))
        wkT = E(nc.sbuf_tensor("wkT", [128, 2048], BF16))
        wvT = E(nc.sbuf_tensor("wvT", [128, 2048], BF16))
        woT = E(nc.sbuf_tensor("woT", [128, 8192], BF16))
        wrT = E(nc.sbuf_tensor("wrT", [128, 8192], BF16))
        wkkT = E(nc.sbuf_tensor("wkkT", [128, 8192], BF16))
        wvvT = E(nc.sbuf_tensor("wvvT", [128, 8192], BF16))
        kf = E(nc.sbuf_tensor("kf", [128, 8192], BF16))
        vp = E(nc.sbuf_tensor("vp", [128, 8320], BF16))
        qsl = E(nc.sbuf_tensor("qsl", [128, 4096], BF16))
        xnw = E(nc.sbuf_tensor("xnw", [128, 4104], BF16))
        masks = E(nc.sbuf_tensor("masks_sb", [128, 896], BF16))
        ident = E(nc.sbuf_tensor("ident_sb", [128, 128], BF16))
        asmb = E(nc.sbuf_tensor("asmb_sb", [128, 128], BF16))
        onesr = E(nc.sbuf_tensor("onesr", [128, 128], BF16))
        xstage = E(nc.sbuf_tensor("xstage", [128, 5120], F32))
        qstage = E(nc.sbuf_tensor("qstage", [128, 1024], BF16))
        xr = E(nc.sbuf_tensor("xr", [128, 2048], BF16))
        xk = E(nc.sbuf_tensor("xk", [128, 2048], BF16))
        kact = E(nc.sbuf_tensor("kact", [128, 2048], BF16))
        mixt = E(nc.sbuf_tensor("mixt", [128, 256], BF16))
        ktmp = E(nc.sbuf_tensor("ktmp", [128, 1024], F32))
        rtok = E(nc.sbuf_tensor("rtok", [128, 1024], F32))
        ffnb = E(nc.sbuf_tensor("ffnb", [128, 1024], F32))
        ffnb2 = E(nc.sbuf_tensor("ffnb2", [128, 1024], F32))
        probs01 = E(nc.sbuf_tensor("probs01", [128, 2048], BF16))
        probs0 = probs01[:, 0:1024]
        probs1 = probs01[:, 1024:2048]
        qsqs = [probs0[:, 0:512], probs0[:, 512:1024]]
        xntok = probs1[:, :]
        ctxf = E(nc.sbuf_tensor("ctxf", [128, 4096], BF16))
        ctxn = E(nc.sbuf_tensor("ctxn", [64, 1024], BF16))
        recip = E(nc.sbuf_tensor("recip", [128, 512], F32))
        rhil = E(nc.sbuf_tensor("rhil", [128, 1024], BF16))
        stt = E(nc.sbuf_tensor("stt", [128, 8], F32))
        sumx = E(nc.sbuf_tensor("sumx", [128, 1], F32))
        sumsq = E(nc.sbuf_tensor("sumsq", [128, 1], F32))
        meanc = E(nc.sbuf_tensor("meanc", [128, 1], F32))
        m2c = E(nc.sbuf_tensor("m2c", [128, 1], F32))
        varc = E(nc.sbuf_tensor("varc", [128, 1], F32))
        lnv = E(nc.sbuf_tensor("lnv", [128, 1], F32))
        rstd = E(nc.sbuf_tensor("rstd", [128, 1], F32))
        nmr = E(nc.sbuf_tensor("nmr", [128, 1], F32))
        epsc = E(nc.sbuf_tensor("epsc", [128, 1], F32))
        negc = E(nc.sbuf_tensor("negc", [128, 1], F32))
        negc1 = E(nc.sbuf_tensor("negc1", [128, 1], BF16))
        murc = E(nc.sbuf_tensor("murc", [128, 8], F32))
        mukc = E(nc.sbuf_tensor("mukc", [128, 8], F32))

        psS01 = E(nc.psum_tensor("psS01", [128, 2048], F32))
        psS0 = psS01[:, 0:1024]
        psS1 = psS01[:, 1024:2048]
        psCA = E(nc.psum_tensor("psCA", [128, 512], F32))
        psCB = E(nc.psum_tensor("psCB", [128, 512], F32))
        psM = E(nc.psum_tensor("psM", [128, 512], F32))
        psT = E(nc.psum_tensor("psT", [128, 1024], BF16))

        sem_names = [
            "s_x0", "s_x1", "s_x2", "s_x3", "s_x4", "s_w",
            "s_wt_q", "s_wt_kv", "s_wt_f", "s_wt_o",
            "s_sm", "s_sm2", "s_q0", "s_q1", "s_qs", "s_p0", "s_p1",
            "s_pl0", "s_pl1", "s_pl2", "s_pl3", "s_out0", "s_out1",
            "a_ln", "d_ln", "c_tp", "c_tpe", "c_proj", "c_pevac", "c_qkv",
            "c_sm", "c_smr", "c_mix", "c_rp", "a_sig", "c_kp", "c_krelu",
            "c_vp", "c_ffnd", "c_sc", "c_exp", "c_msk2", "c_pv",
            "d_att", "p_att", "c_comb", "c_negc", "a_rc", "a_ev",
        ]
        sem_names.append("d_self")
        sem_names.append("a_self")
        sems = {n: E(nc.semaphore(n)) for n in sem_names}
        ctr = {n: 0 for n in sem_names}
        prog = []
        _dself = {"n": 0}
        _aself = {"n": 0}

        def SE(v, inst):
            _dself["n"] += 1
            inst.then_inc(sems["d_self"], 1)
            v.wait_ge(sems["d_self"], _dself["n"])

        def SEA(s, inst):
            _aself["n"] += 1
            inst.then_inc(sems["a_self"], 1)
            s.wait_ge(sems["a_self"], _aself["n"])

        def em(engine, fn):
            prog.append((engine, fn))

        def bump(name, n=1):
            ctr[name] += n
            return ctr[name]

        def pair_of(h):
            a, r = h // 8, h % 8
            return 4 * a + (r % 4), r // 4

        PS = [psS0, psS1]
        XNW = lambda: xnw[:, :].rearrange("p (c w) -> p c w", c=8)

        def setup_gpsimd(g):
            g.dma_start(whq[:, :], wq_d[:, :]).then_inc(sems["s_w"], 16)
            g.dma_start(whk[:, :], wk_d[:, :]).then_inc(sems["s_w"], 16)
            g.dma_start(whv[:, :], wv_d[:, :]).then_inc(sems["s_w"], 16)
            for b in range(16):
                p, slot = b // 2, b % 2
                a, i = p // 4, p % 4
                h = 8 * a + 4 * slot + i
                g.dma_start(who[:, 64 * b:64 * b + 64],
                            wo_d[:, 64 * h:64 * h + 64]).then_inc(sems["s_w"], 16)
            g.dma_start(whr[:, :], wr_d[:, :]).then_inc(sems["s_w"], 16)
            g.dma_start(whkk[:, :], wkk_d[:, :]).then_inc(sems["s_w"], 16)
            g.dma_start(whvv[:, :], wvv_d[:, :]).then_inc(sems["s_w"], 16)
            for j in range(8):
                g.dma_start(murc[:, j:j + 1],
                            mur_d[128 * j:128 * j + 128].rearrange(
                                "(p one) -> p one", p=128, one=1)).then_inc(sems["s_sm"], 16)
                g.dma_start(mukc[:, j:j + 1],
                            muk_d[128 * j:128 * j + 128].rearrange(
                                "(p one) -> p one", p=128, one=1)).then_inc(sems["s_sm"], 16)

        def setup_sync(sy):
            sy.dma_start(masks[:, :], masks_d[:, :]).then_inc(sems["s_sm2"], 16)
            sy.dma_start(ident[:, :], ident_d[:, :]).then_inc(sems["s_sm2"], 16)
            sy.dma_start(asmb[:, :], asmb_d[:, :]).then_inc(sems["s_sm2"], 16)

        def setup_vector(v):
            v.memset(onesr[:, :], 1.0)
            v.memset(epsc[:, :], EPS)
            v.memset(stt[:, :], 0.0)
            v.memset(vp[:, :].rearrange("p (g k w) -> p g k w", g=4, k=32)[:, :, :, 64:65], 1.0)
            v.memset(XNW()[:, :, 0:1], 0.0)

        def emit_xf(src, dst, nchunk, rows, semn):
            for cc in range(nchunk):
                def fn(sy, src=src, dst=dst, cc=cc, rows=rows, semn=semn):
                    sy.dma_start(dst[:, rows * cc:rows * cc + rows],
                                 src[:, 128 * cc:128 * cc + 128],
                                 transpose=True).then_inc(sems[semn], 16)
                bump(semn, 16)
                em("sync", fn)

        def emit_x(t):
            def fn(sy, t=t):
                if t >= 5:
                    sy.wait_ge(sems["c_ffnd"], t - 4)
                sy.dma_start(xstage[:, 1024 * (t % 5):1024 * (t % 5) + 1024],
                             x_d[128 * t:128 * t + 128, :]).then_inc(sems["s_x%d" % (t % 5)], 16)
            bump("s_x%d" % (t % 5), 16)
            em("sync", fn)

        def emit_wait_sw(thr):
            em("sync", lambda sy, thr=thr: sy.wait_ge(sems["s_w"], thr))

        for t in range(4):
            emit_x(t)
        emit_wait_sw(16 * 22)
        emit_xf(whq, wqT, 8, 1024, "s_wt_q")
        emit_xf(whk, wkT, 8, 256, "s_wt_kv")
        emit_xf(whv, wvT, 8, 256, "s_wt_kv")
        emit_xf(whr, wrT, 8, 1024, "s_wt_f")
        emit_xf(whkk, wkkT, 8, 1024, "s_wt_f")
        emit_xf(whvv, wvvT, 8, 1024, "s_wt_f")
        emit_xf(who, woT, 8, 1024, "s_wt_o")
        for t in range(4, 32):
            emit_x(t)

        qevac_cnt = []
        qk_state = {"n": 0, "sm_vals": []}
        pend_stat = []

        def build_ln(g):
            carry = None
            if g >= 1:
                def f_dve_carry(v):
                    SE(v, v.tensor_copy(XNW()[:, :, 0:1], XNW()[:, :, 512:513]))
                carry = ("vector", f_dve_carry)
            tchains = []
            for j in range(4):
                t = 4 * g + j
                tj = []

                c_sm_snap = ctr["c_sm"]

                def f_act_sq(s, t=t, c_sm_snap=c_sm_snap):
                    s.wait_ge(sems["s_x%d" % (t % 5)], 16 * (t // 5 + 1))
                    if c_sm_snap > 0:
                        s.wait_ge(sems["c_sm"], c_sm_snap)
                    if t >= 1:
                        s.wait_ge(sems["d_ln"], 2 * t - 1)
                    s.activation(probs0[:, :], xstage[:, 1024 * (t % 5):1024 * (t % 5) + 1024],
                                 AF.Square, accum_out=sumsq[:, :]).then_inc(sems["a_ln"], 1)
                bump("a_ln")
                tj.append(("scalar", f_act_sq))
                a_sq = ctr["a_ln"]

                def f_dve_stats(v, t=t, a_sq=a_sq):
                    v.wait_ge(sems["s_x%d" % (t % 5)], 16 * (t // 5 + 1))
                    if t >= 1:
                        v.wait_ge(sems["a_ln"], a_sq - 1)
                    SE(v, v.reduce_sum(sumx[:, :],
                                       xstage[:, 1024 * (t % 5):1024 * (t % 5) + 1024],
                                       axis=AX.X))
                    SE(v, v.tensor_scalar(meanc[:, :], sumx[:, :], 1.0 / C, None, ALU.mult))
                    SE(v, v.tensor_tensor(m2c[:, :], meanc[:, :], meanc[:, :], ALU.mult))
                    v.wait_ge(sems["a_ln"], a_sq)
                    v.scalar_tensor_tensor(varc[:, :], sumsq[:, :], 1.0 / C, m2c[:, :],
                                           ALU.mult, ALU.subtract).then_inc(sems["d_ln"], 1)
                bump("d_ln")
                tj.append(("vector", f_dve_stats))
                d_var = ctr["d_ln"]

                def f_act_rstd(s, d_var=d_var):
                    s.wait_ge(sems["d_ln"], d_var)
                    SEA(s, s.activation(lnv[:, :], varc[:, :], AF.Ln, bias=epsc[:, :]))
                    s.activation(rstd[:, :], lnv[:, :], AF.Exp,
                                 scale=-0.5).then_inc(sems["a_ln"], 1)
                bump("a_ln")
                tj.append(("scalar", f_act_rstd))
                a_rstd = ctr["a_ln"]

                def f_dve_nmr(v, a_rstd=a_rstd):
                    v.wait_ge(sems["a_ln"], a_rstd)
                    v.scalar_tensor_tensor(nmr[:, :], meanc[:, :], -1.0, rstd[:, :],
                                           ALU.mult, ALU.mult).then_inc(sems["d_ln"], 1)
                bump("d_ln")
                tj.append(("vector", f_dve_nmr))
                d_nmr = ctr["d_ln"]

                def f_act_apply(s, t=t, d_nmr=d_nmr):
                    s.wait_ge(sems["d_ln"], d_nmr)
                    if t >= 1:
                        s.wait_ge(sems["c_tp"], t)
                    s.activation(xntok[:, :], xstage[:, 1024 * (t % 5):1024 * (t % 5) + 1024],
                                 AF.Identity, bias=nmr[:, :],
                                 scale=rstd[:, :]).then_inc(sems["a_ln"], 1)
                bump("a_ln")
                tj.append(("scalar", f_act_apply))
                a_app = ctr["a_ln"]

                def f_pe_tp(pe, t=t, a_app=a_app):
                    pe.wait_ge(sems["a_ln"], a_app)
                    pe.wait_ge(sems["s_sm2"], SSM_CONST)
                    if t >= 1:
                        pe.wait_ge(sems["c_tpe"], t)
                    for cc in range(8):
                        inst = pe.transpose(psT[:, 128 * cc:128 * cc + 128],
                                            xntok[:, 128 * cc:128 * cc + 128], ident[:, :])
                    inst.then_inc(sems["c_tp"], 1)
                bump("c_tp")
                tj.append(("tensor", f_pe_tp))
                c_tp_t = ctr["c_tp"]

                def f_dve_evac(v, g=g, j=j, c_tp_t=c_tp_t):
                    v.wait_ge(sems["c_tp"], c_tp_t)
                    if j == 0 and g >= 1:
                        v.wait_ge(sems["c_qkv"], g)
                    dst = XNW()[:, :, 1 + 128 * j:1 + 128 * j + 128]
                    src = psT[:, :].rearrange("p (c w) -> p c w", c=8)
                    v.tensor_copy(dst, src).then_inc(sems["c_tpe"], 1)
                bump("c_tpe")
                tj.append(("vector", f_dve_evac))
                tchains.append(tj)
            return carry, tchains

        ln0_carry, ln0_t = build_ln(0)
        for tj in ln0_t:
            for it in tj:
                em(*it)

        for g in range(NG):
            krelu_grp_snap = ctr["c_krelu"]
            c_tpe_g = ctr["c_tpe"]

            mix0_base = ctr["c_mix"]
            mix0_items = []
            c_rp_s0 = ctr["c_rp"]
            c_kp_s0 = ctr["c_kp"]
            for cc in range(8):
                def f_dve_mix0(v, cc=cc, c_tpe_g=c_tpe_g, c_rp_s0=c_rp_s0,
                               c_kp_s0=c_kp_s0, mix0_base=mix0_base):
                    if cc == 0:
                        v.wait_ge(sems["c_tpe"], c_tpe_g)
                        v.wait_ge(sems["s_sm"], SSM_MU)
                        if c_rp_s0 > 0:
                            v.wait_ge(sems["c_rp"], c_rp_s0)
                        if c_kp_s0 > 0:
                            v.wait_ge(sems["c_kp"], c_kp_s0)
                    else:
                        v.wait_ge(sems["c_mix"], mix0_base + 2 * cc)
                    cur = XNW()[:, cc, 1:257]
                    prv = XNW()[:, cc, 0:256]
                    SE(v, v.tensor_tensor(mixt[:, :], cur, prv, ALU.subtract))
                    v.scalar_tensor_tensor(xr[:, 256 * cc:256 * cc + 256], mixt[:, :],
                                           murc[:, cc:cc + 1], prv,
                                           ALU.mult, ALU.add).then_inc(sems["c_mix"], 1)
                    v.scalar_tensor_tensor(xk[:, 256 * cc:256 * cc + 256], mixt[:, :],
                                           mukc[:, cc:cc + 1], prv,
                                           ALU.mult, ALU.add).then_inc(sems["c_mix"], 1)
                bump("c_mix", 2)
                mix0_items.append(("vector", f_dve_mix0))
            mix0_h = ctr["c_mix"]

            group_units = [("q", j) for j in range(8)] + \
                          [("k", a) for a in range(2)] + \
                          [("v", u) for u in range(4)]

            for ui, (kind, idx) in enumerate(group_units):
                unit_no = len(qevac_cnt)
                quarters = [(psS0, 0), (psS0, 512), (psS1, 0), (psS1, 512),
                            (psCA, 0), (psCB, 0)]
                half, hoff = quarters[unit_no % 6]

                def f_pe_proj(pe, g=g, kind=kind, idx=idx, half=half, hoff=hoff,
                              unit_no=unit_no, c_tpe_g=c_tpe_g, ui=ui,
                              krelu_grp_snap=krelu_grp_snap):
                    if unit_no == 0:
                        pe.wait_ge(sems["s_wt_q"], SWT_WQ)
                    if unit_no == 8:
                        pe.wait_ge(sems["s_wt_kv"], SWT_WKV)
                    pe.wait_ge(sems["c_tpe"], c_tpe_g)
                    if ui < 6 and g >= 1:
                        pe.wait_ge(sems["c_ffnd"], 4 * g)
                        pe.wait_ge(sems["c_krelu"], krelu_grp_snap)
                    if unit_no >= 6:
                        pe.wait_ge(sems["c_pevac"], qevac_cnt[unit_no - 6])
                    if kind == "q":
                        for cc in range(8):
                            inst = pe.matmul(half[:, hoff:hoff + 512],
                                      wqT[:, 1024 * cc + 128 * idx:1024 * cc + 128 * idx + 128],
                                      XNW()[:, cc, 1:513],
                                      start=(cc == 0), stop=(cc == 7))
                    elif kind == "k":
                        for cc in range(8):
                            inst = pe.matmul(half[:, hoff:hoff + 512],
                                      wkT[:, 256 * cc + 128 * idx:256 * cc + 128 * idx + 128],
                                      XNW()[:, cc, 1:513],
                                      start=(cc == 0), stop=(cc == 7))
                    else:
                        for cc in range(8):
                            inst = pe.matmul(half[:, hoff:hoff + 256],
                                      XNW()[:, cc, 1 + 128 * idx:1 + 128 * idx + 128],
                                      wvT[:, 256 * cc:256 * cc + 256],
                                      start=(cc == 0), stop=(cc == 7))
                    inst.then_inc(sems["c_proj"], 1)
                bump("c_proj")
                em("tensor", f_pe_proj)
                c_proj_u = ctr["c_proj"]

                if kind in ("q", "k"):
                    jg = 8 * g + idx if kind == "q" else None
                    qk_no = qk_state["n"]
                    qk_state["n"] += 1
                    qsq = qsqs[qk_no % 2]
                    sm_prev = (qk_state["sm_vals"][qk_no - 2]
                               if qk_no >= 2 else 0)
                    c_pev_u = ctr["c_pevac"] + 2

                    def f_dve_pevac(v, g=g, kind=kind, idx=idx, half=half, hoff=hoff,
                                    c_proj_u=c_proj_u, jg=jg, sm_prev=sm_prev,
                                    c_pev_u=c_pev_u, qsq=qsq):
                        v.wait_ge(sems["c_proj"], c_proj_u)
                        if sm_prev > 0:
                            v.wait_ge(sems["c_sm"], sm_prev)
                        if kind == "q":
                            if jg >= 2:
                                v.wait_ge(sems["s_q%d" % (jg % 2)], 32 * (jg // 2))
                            q_half = qstage[:, 512 * (jg % 2):512 * (jg % 2) + 512]
                            v.tensor_scalar(q_half, half[:, hoff:hoff + 512],
                                            0.125, None,
                                            ALU.mult).then_inc(sems["c_pevac"], 1)
                            v.wait_ge(sems["c_pevac"], c_pev_u - 1)
                            v.tensor_tensor(qsq[:, :], q_half, q_half,
                                            ALU.mult).then_inc(sems["c_pevac"], 1)
                        else:
                            kdst = kf[:, 4096 * idx + 512 * g:4096 * idx + 512 * g + 512]
                            v.tensor_copy(kdst,
                                          half[:, hoff:hoff + 512]).then_inc(sems["c_pevac"], 1)
                            v.wait_ge(sems["c_pevac"], c_pev_u - 1)
                            v.tensor_tensor(qsq[:, :], kdst, kdst,
                                            ALU.mult).then_inc(sems["c_pevac"], 1)
                    bump("c_pevac", 2)
                    em("vector", f_dve_pevac)
                    assert c_pev_u == ctr["c_pevac"]
                    c_smr_snap = ctr["c_smr"]
                    c_smr_fin = ctr["c_smr"] + 2

                    def f_pe_stat(pe, c_pev_u=c_pev_u, c_smr_snap=c_smr_snap, qsq=qsq):
                        pe.wait_ge(sems["c_pevac"], c_pev_u)
                        pe.wait_ge(sems["c_smr"], c_smr_snap)
                        pe.matmul(psM[0:1, 0:512], onesr[:, 0:1], qsq[:, :],
                                  start=True, stop=True).then_inc(sems["c_sm"], 1)
                    bump("c_sm")
                    qk_state["sm_vals"].append(ctr["c_sm"])
                    c_sm_u = ctr["c_sm"]
                    col = 0 if kind == "q" else 1

                    def f_dve_stat(v, c_sm_u=c_sm_u, col=col, c_smr_fin=c_smr_fin):
                        v.wait_ge(sems["c_sm"], c_sm_u)
                        v.reduce_max(stt[0:1, 2:3], psM[0:1, 0:512],
                                     axis=AX.X).then_inc(sems["c_smr"], 1)
                        v.wait_ge(sems["c_smr"], c_smr_fin - 1)
                        v.tensor_tensor(stt[0:1, col:col + 1], stt[0:1, col:col + 1],
                                        stt[0:1, 2:3], ALU.max).then_inc(sems["c_smr"], 1)
                        v.wait_ge(sems["c_smr"], c_smr_fin)
                    bump("c_smr", 2)
                    while pend_stat:
                        em(*pend_stat.pop(0))
                    pend_stat.append(("tensor", f_pe_stat))
                    pend_stat.append(("vector", f_dve_stat))

                    if kind == "q":
                        def f_gp_qstore(gp, g=g, idx=idx, jg=jg, c_pev_u=c_pev_u):
                            gp.wait_ge(sems["c_pevac"], c_pev_u)
                            for hh, h in enumerate((2 * idx, 2 * idx + 1)):
                                p, slot = pair_of(h)
                                gp.dma_start(
                                    qscr[p, 64 * slot:64 * slot + 64, 512 * g:512 * g + 512],
                                    qstage[64 * hh:64 * hh + 64,
                                           512 * (jg % 2):512 * (jg % 2) + 512]
                                ).then_inc(sems["s_q%d" % (jg % 2)], 16)
                        bump("s_q%d" % (jg % 2), 32)
                        em("gpsimd", f_gp_qstore)
                else:
                    def f_dve_vevac(v, g=g, idx=idx, half=half, hoff=hoff, c_proj_u=c_proj_u):
                        v.wait_ge(sems["c_proj"], c_proj_u)
                        kj = 4 * g + idx
                        dst = vp[:, :].rearrange("p (gg k w) -> p gg k w",
                                                 gg=4, k=32)[:, :, kj, 0:64]
                        src = half[:, hoff:hoff + 256].rearrange("p (gg w) -> p gg w", gg=4)
                        v.tensor_copy(dst, src).then_inc(sems["c_pevac"], 1)
                    bump("c_pevac")
                    em("vector", f_dve_vevac)
                qevac_cnt.append(ctr["c_pevac"])
                if mix0_items:
                    em(*mix0_items.pop(0))

            grp_evac_all = ctr["c_pevac"]
            while mix0_items:
                em(*mix0_items.pop(0))
            while pend_stat:
                em(*pend_stat.pop(0))

            def f_pe_qkv_done(pe):
                pe.nop().then_inc(sems["c_qkv"], 1)
            bump("c_qkv")
            em("tensor", f_pe_qkv_done)

            ln_next = build_ln(g + 1) if g + 1 < NG else None
            for hg in range(2):
                spliced = ln_next is not None and hg == 1
                LT = ln_next[1] if ln_next is not None else None
                c_rp_snap = ctr["c_rp"]
                c_kp_snap = ctr["c_kp"]
                if hg == 0:
                    c_mix_h = mix0_h
                    c_vp_half_snap = ctr["c_vp"]
                    do_mix = False
                else:
                    do_mix = True
                if do_mix:
                    def f_dve_mix(v, g=g, hg=hg, c_tpe_g=c_tpe_g,
                                  c_rp_snap=c_rp_snap, c_kp_snap=c_kp_snap,
                                  c_mix_h=ctr["c_mix"] + 16):
                        v.wait_ge(sems["c_tpe"], c_tpe_g)
                        v.wait_ge(sems["s_sm"], SSM_MU)
                        if c_rp_snap > 0:
                            v.wait_ge(sems["c_rp"], c_rp_snap)
                        if c_kp_snap > 0:
                            v.wait_ge(sems["c_kp"], c_kp_snap)
                        for cc in range(8):
                            cur = XNW()[:, cc, 1 + 256 * hg:1 + 256 * hg + 256]
                            prv = XNW()[:, cc, 0 + 256 * hg:0 + 256 * hg + 256]
                            if cc >= 1:
                                v.wait_ge(sems["c_mix"], c_mix_h - 16 + 2 * cc)
                            SE(v, v.tensor_tensor(mixt[:, :], cur, prv, ALU.subtract))
                            v.scalar_tensor_tensor(xr[:, 256 * cc:256 * cc + 256], mixt[:, :],
                                                   murc[:, cc:cc + 1], prv,
                                                   ALU.mult, ALU.add).then_inc(sems["c_mix"], 1)
                            v.scalar_tensor_tensor(xk[:, 256 * cc:256 * cc + 256], mixt[:, :],
                                                   mukc[:, cc:cc + 1], prv,
                                                   ALU.mult, ALU.add).then_inc(sems["c_mix"], 1)
                    bump("c_mix", 16)
                    em("vector", f_dve_mix)
                    if spliced and ln_next[0] is not None:
                        em(*ln_next[0])
                    c_mix_h = ctr["c_mix"]
                    c_vp_half_snap = ctr["c_vp"]

                for kw in range(2):
                    c_krelu_snap = ctr["c_krelu"]

                    def f_pe_k(pe, g=g, hg=hg, kw=kw, c_mix_h=c_mix_h,
                               c_krelu_snap=c_krelu_snap, grp_evac_all=grp_evac_all):
                        if g == 0 and hg == 0 and kw == 0:
                            pe.wait_ge(sems["s_wt_f"], SWT_WF)
                        if kw == 0:
                            pe.wait_ge(sems["c_pevac"], grp_evac_all)
                        if c_krelu_snap > 0:
                            pe.wait_ge(sems["c_krelu"], c_krelu_snap - 1)
                        ktiles = [psS1[:, 0:256], psS1[:, 512:768],
                                  psCA[:, 0:256], psCB[:, 0:256]]
                        for cc in range(8):
                            pe.wait_ge(sems["c_mix"], c_mix_h - 16 + 2 * cc + 2)
                            for dk4 in range(4):
                                dk = 4 * kw + dk4
                                inst = pe.matmul(ktiles[dk4],
                                          wkkT[:, 1024 * cc + 128 * dk:1024 * cc + 128 * dk + 128],
                                          xk[:, 256 * cc:256 * cc + 256],
                                          start=(cc == 0), stop=(cc == 7))
                        inst.then_inc(sems["c_kp"], 1)
                    bump("c_kp")
                    if spliced:
                        em(*LT[kw][0])
                        em(*LT[kw][2])
                    em("tensor", f_pe_k)
                    c_kp_w = ctr["c_kp"]

                    krelu_fin = ctr["c_krelu"] + 5

                    def f_dve_krelu(v, kw=kw, c_kp_w=c_kp_w, c_vp_half_snap=c_vp_half_snap,
                                    krelu_fin=krelu_fin):
                        v.wait_ge(sems["c_kp"], c_kp_w)
                        if krelu_fin > 5:
                            v.wait_ge(sems["c_krelu"], krelu_fin - 5)
                        if c_vp_half_snap > 0:
                            v.wait_ge(sems["c_vp"], c_vp_half_snap)
                        ktiles = [psS1[:, 0:256], psS1[:, 512:768],
                                  psCA[:, 0:256], psCB[:, 0:256]]
                        for qd in range(4):
                            v.tensor_scalar(ktmp[:, 256 * qd:256 * qd + 256], ktiles[qd],
                                            0.0, None,
                                            ALU.max).then_inc(sems["c_krelu"], 1)
                        v.wait_ge(sems["c_krelu"], krelu_fin - 1)
                        v.tensor_tensor(kact[:, 1024 * kw:1024 * kw + 1024],
                                        ktmp[:, :], ktmp[:, :],
                                        ALU.mult).then_inc(sems["c_krelu"], 1)
                    bump("c_krelu", 5)
                    em("vector", f_dve_krelu)
                    if spliced:
                        em(*LT[kw][1])
                        em(*LT[kw][3])
                        em(*LT[kw][4])
                if spliced:
                    em(*LT[0][5])
                c_krelu_h = ctr["c_krelu"]

                for mm in range(2):
                    m = 2 * hg + mm
                    mi = 4 * g + m

                    def f_pe_r(pe, g=g, m=m, mm=mm, mi=mi, c_mix_h=c_mix_h,
                               grp_evac_all=grp_evac_all):
                        if g == 0 and m == 0:
                            pe.wait_ge(sems["s_wt_f"], SWT_WF)
                        if m == 0:
                            pe.wait_ge(sems["c_pevac"], grp_evac_all)
                        if mi >= 1:
                            pe.wait_ge(sems["c_ffnd"], mi)
                        for cc in range(8):
                            pe.wait_ge(sems["c_mix"], c_mix_h - 16 + 2 * cc + 1)
                            for hh in range(2):
                                inst = pe.matmul(psS0[:, 512 * hh:512 * hh + 512],
                                          xr[:, 256 * cc + 128 * mm:256 * cc + 128 * mm + 128],
                                          wrT[:, 1024 * cc + 512 * hh:1024 * cc + 512 * hh + 512],
                                          start=(cc == 0), stop=(cc == 7))
                        inst.then_inc(sems["c_rp"], 1)
                    bump("c_rp")
                    em("tensor", f_pe_r)
                    if spliced and mm == 0:
                        em(*LT[1][5])
                        em(*LT[0][6])
                    c_rp_m = ctr["c_rp"]

                    def f_act_sig(s, c_rp_m=c_rp_m, a1=ctr["a_sig"] + 1):
                        s.wait_ge(sems["c_rp"], c_rp_m)
                        s.activation(rtok[:, :], psS0[:, :], AF.Exp,
                                     scale=-1.0).then_inc(sems["a_sig"], 1)
                        s.wait_ge(sems["a_sig"], a1)
                        SEA(s, s.activation(rtok[:, :], rtok[:, :], AF.Ln,
                                            bias=1.0))
                        s.activation(rtok[:, :], rtok[:, :], AF.Exp,
                                     scale=-1.0).then_inc(sems["a_sig"], 1)
                    bump("a_sig", 2)
                    em("scalar", f_act_sig)
                    a_sig_m = ctr["a_sig"] - 1
                    if spliced:
                        if mm == 0:
                            em(*LT[1][6])
                        tn = 2 + mm
                        for it in (LT[tn][0], LT[tn][2], LT[tn][1], LT[tn][3],
                                   LT[tn][4]):
                            em(*it)
                    a_sig2_m = ctr["a_sig"]

                    def f_pe_vout(pe, g=g, mi=mi, mm=mm, c_krelu_h=c_krelu_h,
                                  a_sig_m=a_sig_m):
                        if g == 0 and mi == 0:
                            pe.wait_ge(sems["s_wt_f"], SWT_WF)
                        pe.wait_ge(sems["c_krelu"], c_krelu_h)
                        pe.wait_ge(sems["a_sig"], a_sig_m)
                        for hh in range(2):
                            for dk in range(8):
                                inst = pe.matmul(psS0[:, 512 * hh:512 * hh + 512],
                                          kact[:, 256 * dk + 128 * mm:256 * dk + 128 * mm + 128],
                                          wvvT[:, 1024 * dk + 512 * hh:1024 * dk + 512 * hh + 512],
                                          start=(dk == 0), stop=(dk == 7))
                        inst.then_inc(sems["c_vp"], 1)
                    bump("c_vp")
                    em("tensor", f_pe_vout)
                    if spliced:
                        em(*LT[2 + mm][5])
                    c_vp_m = ctr["c_vp"]
                    fb = ffnb if mi % 2 == 0 else ffnb2

                    def f_dve_ffn(v, mi=mi, fb=fb, c_vp_m=c_vp_m, a_sig2_m=a_sig2_m):
                        v.wait_ge(sems["c_vp"], c_vp_m)
                        v.wait_ge(sems["a_sig"], a_sig2_m)
                        if mi >= 2:
                            v.wait_ge(sems["s_p%d" % (mi % 2)], 16 * (mi // 2))
                        SE(v, v.tensor_tensor(fb[:, :], psS0[:, :], rtok[:, :], ALU.mult))
                        v.tensor_tensor(fb[:, :], fb[:, :],
                                        xstage[:, 1024 * (mi % 5):1024 * (mi % 5) + 1024],
                                        ALU.add).then_inc(sems["c_ffnd"], 1)
                    bump("c_ffnd")
                    em("vector", f_dve_ffn)
                    c_ffnd_m = ctr["c_ffnd"]

                    def f_gp_pstore(gp, mi=mi, fb=fb, c_ffnd_m=c_ffnd_m):
                        gp.wait_ge(sems["c_ffnd"], c_ffnd_m)
                        gp.dma_start(out_d[128 * mi:128 * mi + 128, :],
                                     fb[:, :]).then_inc(sems["s_p%d" % (mi % 2)], 16)
                    bump("s_p%d" % (mi % 2), 16)
                    em("gpsimd", f_gp_pstore)
                    if spliced:
                        em(*LT[2 + mm][6])

        def f_dve_cprep(v):
            v.tensor_tensor(stt[0:1, 3:4], stt[0:1, 0:1], stt[0:1, 1:2],
                            ALU.mult).then_inc(sems["d_att"], 1)
        bump("d_att")
        em("vector", f_dve_cprep)
        d_cp = ctr["d_att"]

        def f_act_c(s, d_cp=d_cp):
            s.wait_ge(sems["d_att"], d_cp)
            SEA(s, s.activation(stt[0:1, 4:5], stt[0:1, 3:4], AF.Ln))
            s.activation(stt[0:1, 5:6], stt[0:1, 4:5], AF.Exp,
                         scale=0.5).then_inc(sems["a_ln"], 1)
        bump("a_ln")
        em("scalar", f_act_c)
        a_c = ctr["a_ln"]

        def f_dve_negc1(v, a_c=a_c):
            v.wait_ge(sems["a_ln"], a_c)
            v.tensor_scalar(negc1[0:1, 0:1], stt[0:1, 5:6], -1.0, -0.5,
                            ALU.mult, ALU.add).then_inc(sems["d_att"], 1)
        bump("d_att")
        em("vector", f_dve_negc1)
        d_nc1 = ctr["d_att"]

        def f_pe_negc(pe, d_nc1=d_nc1):
            pe.wait_ge(sems["d_att"], d_nc1)
            pe.matmul(psM[0:128, 0:1], onesr[0:1, :], negc1[0:1, 0:1],
                      start=True, stop=True).then_inc(sems["p_att"], 1)
        bump("p_att")
        em("tensor", f_pe_negc)
        p_nc = ctr["p_att"]

        def f_dve_negc(v, p_nc=p_nc):
            v.wait_ge(sems["p_att"], p_nc)
            v.tensor_copy(negc[:, :], psM[:, 0:1]).then_inc(sems["c_negc"], 1)
        bump("c_negc")
        em("vector", f_dve_negc)

        PH2 = BUILD_PHASE >= 2
        wave = 0
        pair_gl = 0
        pend_lazy = []
        d_evac_prev = [0]
        d_norm_prev = [0]
        exp_at_wave = []
        pv_at_wave = []

        pend_epi = []
        sc_pair_snap = []
        for sg in range(8 if PH2 else 0):
            for pr in range(8):
                qs_prev = ctr["s_qs"]

                def f_sy_qsl(sy, sg=sg, pr=pr, qs_prev=qs_prev,
                             sc_snap=(sc_pair_snap[8 * (sg - 1) + pr] if sg >= 1 else 0)):
                    if sg == 0 and pr == 0:
                        sy.wait_ge(sems["s_q0"], 32 * 32)
                        sy.wait_ge(sems["s_q1"], 32 * 32)
                    if qs_prev > 0:
                        sy.wait_ge(sems["s_qs"], qs_prev)
                    if sg >= 1:
                        sy.wait_ge(sems["c_sc"], sc_snap)
                    sy.dma_start(
                        qsl[:, 512 * pr:512 * pr + 512],
                        qscr[pr:pr + 1, :, 512 * sg:512 * sg + 512].rearrange(
                            "p r c -> r (p c)")).then_inc(sems["s_qs"], 16)
                bump("s_qs", 16)
                em("sync", f_sy_qsl)
            s_qs_sg = ctr["s_qs"]

            for u in range(4):
                def f_sy_pl(sy, sg=sg, u=u):
                    if sg == 0 and u == 0:
                        sy.wait_ge(sems["s_p0"], 16 * 16)
                        sy.wait_ge(sems["s_p1"], 16 * 16)
                        sy.wait_ge(sems["c_ffnd"], 32)
                    if sg >= 1:
                        sy.wait_ge(sems["c_comb"], 4 * (sg - 1) + u + 1)
                    sy.dma_start(xstage[:, 1024 * u:1024 * u + 1024],
                                 out_d[128 * (4 * sg + u):128 * (4 * sg + u) + 128, :]
                                 ).then_inc(sems["s_pl%d" % u], 16)
                bump("s_pl%d" % u, 16)
                em("sync", f_sy_pl)

            asme_vals = []
            for p in range(8):
                a = p // 4
                gA, gB = 2 * a, 2 * a + 1
                nkj = 4 * sg + 4
                d_att_pair = d_evac_prev[0]
                sc_fns = []
                pv_fns = []
                exp_fns = []
                msk_fns = []
                for kj in range(nkj):
                    w = wave
                    pi = w % 2
                    diag = kj >= 4 * sg
                    dd = 128 * kj - 512 * sg if diag else 0

                    def f_pe_sc(pe, w=w, pi=pi, p=p, a=a, kj=kj, sg=sg,
                                s_qs_sg=s_qs_sg):
                        if w == 0:
                            pe.wait_ge(sems["c_negc"], 1)
                        pe.wait_ge(sems["s_qs"], 16 * (8 * sg + p + 1))
                        if w >= 2:
                            pe.wait_ge(sems["c_exp"], exp_at_wave[w - 2])
                        if p == 0 and kj <= 1 and sg >= 1:
                            pe.wait_ge(sems["c_comb"], 4 * (sg - 1) + 3 + pi)
                        ps = PS[pi]
                        pe.matmul(ps[:, 0:512],
                                  kf[0:64, 4096 * a + 128 * kj:4096 * a + 128 * kj + 128],
                                  qsl[0:64, 512 * p:512 * p + 512], start=True, stop=True)
                        pe.matmul(ps[:, 512:1024],
                                  kf[64:128, 4096 * a + 128 * kj:4096 * a + 128 * kj + 128],
                                  qsl[64:128, 512 * p:512 * p + 512],
                                  start=True, stop=True).then_inc(sems["c_sc"], 1)
                    bump("c_sc")
                    sc_fns.append(f_pe_sc)
                    c_sc_w = ctr["c_sc"]

                    def f_act_exp(s, w=w, pi=pi, c_sc_w=c_sc_w):
                        if w == 0:
                            s.wait_ge(sems["c_negc"], 1)
                        s.wait_ge(sems["c_sc"], c_sc_w)
                        if w >= 2:
                            s.wait_ge(sems["c_pv"], pv_at_wave[w - 2])
                        pb = probs0 if w % 2 == 0 else probs1
                        s.activation(pb[:, :], PS[pi][:, :], AF.Exp,
                                     bias=negc[:, :]).then_inc(sems["c_exp"], 1)
                    bump("c_exp")
                    exp_fns.append(f_act_exp)
                    exp_at_wave.append(ctr["c_exp"])
                    c_exp_w = ctr["c_exp"]

                    if diag:
                        def f_dve_mask(v, w=w, dd=dd, c_exp_w=c_exp_w):
                            v.wait_ge(sems["c_exp"], c_exp_w)
                            if w < 8:
                                v.wait_ge(sems["s_sm2"], SSM_CONST)
                            pb = probs0 if w % 2 == 0 else probs1
                            mv = masks[:, 384 - dd:384 - dd + 512]
                            v.tensor_tensor(pb[:, 0:512], pb[:, 0:512], mv,
                                            ALU.mult).then_inc(sems["c_msk2"], 1)
                            v.tensor_tensor(pb[:, 512:1024], pb[:, 512:1024], mv,
                                            ALU.mult).then_inc(sems["c_msk2"], 1)
                        bump("c_msk2", 2)
                        msk_fns.append(f_dve_mask)
                    else:
                        msk_fns.append(None)
                    c_msk_w = ctr["c_msk2"]

                    def f_pe_pv(pe, w=w, kj=kj, gA=gA, gB=gB, diag=diag,
                                c_exp_w=c_exp_w, c_msk_w=c_msk_w, first=(kj == 0),
                                last=(kj == nkj - 1), d_att_pair=d_att_pair):
                        if diag:
                            pe.wait_ge(sems["c_msk2"], c_msk_w)
                        else:
                            pe.wait_ge(sems["c_exp"], c_exp_w)
                        if first:
                            if isinstance(d_att_pair, tuple):
                                pe.wait_ge(sems["d_att"], d_att_pair[0])
                                pe.wait_ge(sems["a_ev"], d_att_pair[1])
                            elif d_att_pair > 0:
                                pe.wait_ge(sems["d_att"], d_att_pair)
                        pb = probs0 if w % 2 == 0 else probs1
                        pe.matmul(psCA[0:65, :],
                                  vp[:, 2080 * gA + 65 * kj:2080 * gA + 65 * kj + 65],
                                  pb[:, 0:512], start=first, stop=last)
                        pe.matmul(psCB[0:65, :],
                                  vp[:, 2080 * gB + 65 * kj:2080 * gB + 65 * kj + 65],
                                  pb[:, 512:1024], start=first,
                                  stop=last).then_inc(sems["c_pv"], 1)
                    bump("c_pv")
                    pv_fns.append(f_pe_pv)
                    pv_at_wave.append(ctr["c_pv"])
                    wave += 1
                sc_pair_snap.append(ctr["c_sc"])

                em("tensor", sc_fns[0])
                if nkj > 1:
                    em("tensor", sc_fns[1])
                for kj in range(nkj):
                    em("scalar", exp_fns[kj])
                    if msk_fns[kj] is not None:
                        em("vector", msk_fns[kj])
                    em("tensor", pv_fns[kj])
                    if kj + 2 < nkj:
                        em("tensor", sc_fns[kj + 2])
                    if pend_lazy:
                        em(*pend_lazy.pop(0))
                while pend_lazy:
                    em(*pend_lazy.pop(0))

                c_pv_last = ctr["c_pv"]

                def f_dve_ctxevac(v, c_pv_last=c_pv_last,
                                  d_prev_norm=d_norm_prev[0]):
                    v.wait_ge(sems["c_pv"], c_pv_last)
                    if d_prev_norm > 0:
                        v.wait_ge(sems["d_att"], d_prev_norm)
                    SE(v, v.tensor_copy(ctxn[:, 0:512], psCA[0:64, 0:512]))
                    v.tensor_copy(ctxn[:, 512:1024],
                                  psCB[0:64, 0:512]).then_inc(sems["d_att"], 1)
                bump("d_att")
                em("vector", f_dve_ctxevac)
                d_evac = ctr["d_att"]

                def f_act_denevac(s, c_pv_last=c_pv_last,
                                  d_prev_norm=d_norm_prev[0]):
                    s.wait_ge(sems["c_pv"], c_pv_last)
                    if d_prev_norm > 0:
                        s.wait_ge(sems["d_att"], d_prev_norm)
                    SEA(s, s.activation(recip[64:65, 0:512], psCA[64:65, 0:512],
                                        AF.Identity))
                    s.activation(ktmp[64:65, 512:1024], psCB[64:65, 0:512],
                                 AF.Identity).then_inc(sems["a_ev"], 1)
                bump("a_ev")
                em("scalar", f_act_denevac)
                a_ev_p = ctr["a_ev"]
                d_evac_prev[0] = (d_evac, a_ev_p)

                for hh in range(2):
                    p_prev = ctr["p_att"]
                    d_prev_chain = ctr["d_att"]

                    if sg < 4 or p == 7:
                        def f_act_rcp(s, hh=hh, a_ev_p=a_ev_p, p_prev=p_prev,
                                      d_prev_chain=d_prev_chain):
                            s.wait_ge(sems["a_ev"], a_ev_p)
                            if hh == 0:
                                src_row = recip[64:65, 0:512]
                            else:
                                s.wait_ge(sems["d_att"], d_prev_chain)
                                src_row = ktmp[64:65, 512:1024]
                            SEA(s, s.activation(ktmp[64:65, 0:512], src_row, AF.Ln))
                            s.activation(recip[64:65, 0:512], ktmp[64:65, 0:512],
                                         AF.Exp, scale=-1.0).then_inc(sems["a_rc"], 1)
                        bump("a_rc")
                        pend_lazy.append(("scalar", f_act_rcp))
                        a_rc_h = ctr["a_rc"]

                        def f_dve_hilo(v, a_rc_h=a_rc_h):
                            v.wait_ge(sems["a_rc"], a_rc_h)
                            SE(v, v.tensor_copy(rhil[64:65, 0:512], recip[64:65, 0:512]))
                            v.tensor_tensor(rhil[64:65, 512:1024], recip[64:65, 0:512],
                                            rhil[64:65, 0:512],
                                            ALU.subtract).then_inc(sems["d_att"], 1)
                        bump("d_att")
                        pend_lazy.append(("vector", f_dve_hilo))
                    else:
                        def f_dve_rcp(v, hh=hh, a_ev_p=a_ev_p, p_prev=p_prev,
                                      d_prev_chain=d_prev_chain):
                            v.wait_ge(sems["a_ev"], a_ev_p)
                            if hh == 0:
                                src_row = recip[64:65, 0:512]
                            else:
                                v.wait_ge(sems["d_att"], d_prev_chain)
                                src_row = ktmp[64:65, 512:1024]
                            SE(v, v.reciprocal(recip[64:65, 0:512], src_row))
                            SE(v, v.tensor_copy(rhil[64:65, 0:512], recip[64:65, 0:512]))
                            v.tensor_tensor(rhil[64:65, 512:1024], recip[64:65, 0:512],
                                            rhil[64:65, 0:512],
                                            ALU.subtract).then_inc(sems["d_att"], 1)
                        bump("d_att")
                        pend_lazy.append(("vector", f_dve_rcp))
                    d_rcp_h = ctr["d_att"]

                    def f_pe_bc(pe, d_rcp_h=d_rcp_h):
                        pe.wait_ge(sems["d_att"], d_rcp_h)
                        pe.matmul(psM[0:64, 0:512], onesr[64:65, 0:64],
                                  rhil[64:65, 0:512], start=True, stop=False)
                        pe.matmul(psM[0:64, 0:512], onesr[64:65, 0:64],
                                  rhil[64:65, 512:1024], start=False,
                                  stop=True).then_inc(sems["p_att"], 1)
                    bump("p_att")
                    pend_lazy.append(("tensor", f_pe_bc))
                    p_bc = ctr["p_att"]

                    def f_dve_norm(v, hh=hh, p_bc=p_bc):
                        v.wait_ge(sems["p_att"], p_bc)
                        v.tensor_tensor(ctxn[:, 512 * hh:512 * hh + 512],
                                        ctxn[:, 512 * hh:512 * hh + 512],
                                        psM[0:64, 0:512],
                                        ALU.mult).then_inc(sems["d_att"], 1)
                    bump("d_att")
                    pend_lazy.append(("vector", f_dve_norm))
                d_norm = ctr["d_att"]
                d_norm_prev[0] = d_norm
                first_pair = (pair_gl == 0)

                def f_pe_asm(pe, d_norm=d_norm, first_pair=first_pair):
                    if first_pair:
                        pe.wait_ge(sems["s_sm2"], SSM_CONST)
                    pe.wait_ge(sems["d_att"], d_norm)
                    pe.matmul(psM[:, 0:512], ident[0:64, :], ctxn[:, 0:512],
                              start=True, stop=False)
                    pe.matmul(psM[:, 0:512], asmb[0:64, :], ctxn[:, 512:1024],
                              start=False, stop=True).then_inc(sems["p_att"], 1)
                bump("p_att")
                pend_lazy.append(("tensor", f_pe_asm))
                p_asm = ctr["p_att"]

                def f_dve_asme(v, p=p, p_asm=p_asm):
                    v.wait_ge(sems["p_att"], p_asm)
                    v.tensor_copy(ctxf[:, 512 * p:512 * p + 512],
                                  psM[:, 0:512]).then_inc(sems["d_att"], 1)
                bump("d_att")
                pend_lazy.append(("vector", f_dve_asme))
                asme_vals.append(ctr["d_att"])
                pair_gl += 1

            d_ctxf = ctr["d_att"]
            exp_all = ctr["c_exp"]
            c_comb_sg0 = ctr["c_comb"]
            n7 = 10 if sg < 4 else 8
            while len(pend_lazy) > n7:
                em(*pend_lazy.pop(0))

            def mk_wo_head(u, d_p6=asme_vals[6], exp_all=exp_all):
                def f_pe_wo_head(pe, u=u, d_p6=d_p6, exp_all=exp_all):
                    ps = PS[u % 2]
                    if u == 0:
                        pe.wait_ge(sems["s_wt_o"], SWT_WO)
                    pe.wait_ge(sems["c_exp"], exp_all)
                    pe.wait_ge(sems["d_att"], d_p6)
                    for p in range(7):
                        for hh in range(2):
                            pe.matmul(ps[:, 512 * hh:512 * hh + 512],
                                      ctxf[:, 512 * p + 128 * u:512 * p + 128 * u + 128],
                                      woT[:, 1024 * p + 512 * hh:1024 * p + 512 * hh + 512],
                                      start=(p == 0), stop=False)
                return f_pe_wo_head

            em("tensor", mk_wo_head(0))
            for _ in range(n7 // 2 - 1):
                em(*pend_lazy.pop(0))
            em("tensor", mk_wo_head(1))
            while pend_lazy:
                em(*pend_lazy.pop(0))
            d_ctxf = ctr["d_att"]
            for u in range(4):
                ps = PS[u % 2]

                def f_pe_wo(pe, u=u, ps=ps, d_ctxf=d_ctxf, exp_all=exp_all,
                            c_comb_sg0=c_comb_sg0):
                    if u < 2:
                        pe.wait_ge(sems["d_att"], d_ctxf)
                        for hh in range(2):
                            inst = pe.matmul(ps[:, 512 * hh:512 * hh + 512],
                                      ctxf[:, 512 * 7 + 128 * u:512 * 7 + 128 * u + 128],
                                      woT[:, 1024 * 7 + 512 * hh:1024 * 7 + 512 * hh + 512],
                                      start=False, stop=True)
                    else:
                        pe.wait_ge(sems["c_comb"], c_comb_sg0 + u - 1)
                        for hh in range(2):
                            for p in range(8):
                                inst = pe.matmul(ps[:, 512 * hh:512 * hh + 512],
                                          ctxf[:, 512 * p + 128 * u:512 * p + 128 * u + 128],
                                          woT[:, 1024 * p + 512 * hh:1024 * p + 512 * hh + 512],
                                          start=(p == 0), stop=(p == 7))
                    inst.then_inc(sems["p_att"], 1)
                bump("p_att")
                em("tensor", f_pe_wo)
                p_wo = ctr["p_att"]
                ob = ffnb if u % 2 == 0 else ffnb2

                def f_dve_comb(v, sg=sg, u=u, ps=ps, ob=ob, p_wo=p_wo):
                    v.wait_ge(sems["p_att"], p_wo)
                    v.wait_ge(sems["s_pl%d" % u], 16 * (sg + 1))
                    cu = 4 * sg + u
                    if cu >= 2:
                        v.wait_ge(sems["s_out%d" % (cu % 2)], 16 * (cu // 2))
                    v.tensor_tensor(ob[:, :], ps[:, :],
                                    xstage[:, 1024 * u:1024 * u + 1024],
                                    ALU.add).then_inc(sems["c_comb"], 1)
                bump("c_comb")
                em("vector", f_dve_comb)
                c_comb_u = ctr["c_comb"]

                def f_gp_ostore(gp, sg=sg, u=u, ob=ob, c_comb_u=c_comb_u):
                    gp.wait_ge(sems["c_comb"], c_comb_u)
                    gp.dma_start(out_d[128 * (4 * sg + u):128 * (4 * sg + u) + 128, :],
                                 ob[:, :]).then_inc(sems["s_out%d" % ((4 * sg + u) % 2)], 16)
                bump("s_out%d" % ((4 * sg + u) % 2), 16)
                em("gpsimd", f_gp_ostore)

        def f_gp_fin(gp):
            if PH2:
                gp.wait_ge(sems["s_out0"], 16 * 16)
                gp.wait_ge(sems["s_out1"], 16 * 16)
            else:
                gp.wait_ge(sems["s_p0"], 16 * 16)
                gp.wait_ge(sems["s_p1"], 16 * 16)
                gp.wait_ge(sems["s_q0"], 32 * 32)
                gp.wait_ge(sems["s_q1"], 32 * 32)
                gp.wait_ge(sems["s_wt_o"], SWT_WO)
        em("gpsimd", f_gp_fin)

        blk = nc.Block()
        blk.__enter__()
        blk.gpsimd(setup_gpsimd)
        blk.sync(setup_sync)
        blk.vector(setup_vector)
        blk.__exit__(None, None, None)

        blk = nc.Block()
        blk.__enter__()
        for eng in ("sync", "tensor", "vector", "scalar", "gpsimd"):
            fns = [fn for e, fn in prog if e == eng]

            def section(engine, fns=fns):
                for fn in fns:
                    fn(engine)

            getattr(blk, eng)(section)
        blk.__exit__(None, None, None)

    return nc


_NC_CACHE = [None]


def _consts():
    t = np.arange(128).reshape(128, 1)
    u = np.arange(896).reshape(1, 896)
    masks = (t <= u - 384).astype(ml_dtypes.bfloat16)
    ident = np.eye(128, dtype=ml_dtypes.bfloat16)
    asmb = np.zeros((128, 128), dtype=ml_dtypes.bfloat16)
    for k in range(64):
        asmb[k, 64 + k] = 1
    return masks, ident, asmb


def kernel(L0_attn_x, L0_attn_wq, L0_attn_wk, L0_attn_wv, L0_attn_wo,
           L0_ffn_W_r, L0_ffn_W_k, L0_ffn_W_v, L0_ffn_mu_r, L0_ffn_mu_k):
    x = np.ascontiguousarray(np.asarray(L0_attn_x, dtype=np.float32))
    masks, ident, asmb = _consts()
    shared = dict(
        wq=np.ascontiguousarray(np.asarray(L0_attn_wq, np.float32)),
        wk=np.ascontiguousarray(np.asarray(L0_attn_wk, np.float32)),
        wv=np.ascontiguousarray(np.asarray(L0_attn_wv, np.float32)),
        wo=np.ascontiguousarray(np.asarray(L0_attn_wo, np.float32)),
        wr=np.ascontiguousarray(np.asarray(L0_ffn_W_r, np.float32)),
        wkk=np.ascontiguousarray(np.asarray(L0_ffn_W_k, np.float32)),
        wvv=np.ascontiguousarray(np.asarray(L0_ffn_W_v, np.float32)),
        mur=np.ascontiguousarray(np.asarray(L0_ffn_mu_r, np.float32)),
        muk=np.ascontiguousarray(np.asarray(L0_ffn_mu_k, np.float32)),
        masks=masks, ident=ident, asmb=asmb,
    )
    if _NC_CACHE[0] is None:
        _NC_CACHE[0] = build_nc()
    nc = _NC_CACHE[0]
    in_maps = [dict(shared, x=np.ascontiguousarray(x[b])) for b in range(8)]
    res = run_bass_kernel_spmd(nc, in_maps, core_ids=list(range(8)))
    out = np.stack([np.asarray(r["out"], dtype=np.float32) for r in res.results], axis=0)
    return out
